# version 25
# baseline (speedup 1.0000x reference)
import contextlib
import numpy as np

import concourse.bass as bass
import concourse.mybir as mybir
from concourse.bass_utils import run_bass_kernel_spmd

F32 = mybir.dt.float32
FP16 = mybir.dt.float16
BF16 = mybir.dt.bfloat16
AF = mybir.ActivationFunctionType
OP = mybir.AluOpType

P = 128
B_LOC = 2
L = 2048
DM = 1024
DIN = 2048
NST = 16
H = 4096
T = 256
NCH = L // T
NDT = DIN // P
NKD = DM // P
NKH = H // P
NMH = H // P
EPS = 1e-5


class Ev:

    def __init__(self, nc, known=None):
        self.nc = nc
        self.sems = {}
        self.cnt = {}
        self.events = {}
        self.known = known

    def sem(self, name):
        if name not in self.sems:
            self.sems[name] = self.nc.alloc_semaphore(name)
            self.cnt[name] = 0
        return self.sems[name]

    def inc(self, instr, name, amt):
        s = self.sem(name)
        self.cnt[name] += amt
        instr.then_inc(s, amt)
        return self.cnt[name]

    def mark(self, ev_name, sem_name):
        self.sem(sem_name)
        self.events[ev_name] = (sem_name, self.cnt[sem_name])

    def wait(self, eng, ev_name):
        tbl = self.known if self.known is not None else self.events
        if ev_name not in tbl:
            if self.known is not None:
                raise KeyError(f"unknown event {ev_name}")
            return
        sname, val = tbl[ev_name]
        if val > 0:
            eng.wait_ge(self.sem(sname), val)


def build_nc(known_events=None, n_iters=B_LOC * NCH, do_a=True, do_b=True):
    nc = bass.Bass("TRN2", target_bir_lowering=False)

    x_in = nc.dram_tensor("x", [B_LOC, L, DM], F32, kind="ExternalInput").ap()
    w_in = nc.dram_tensor("W_in", [2 * DIN, DM], F32, kind="ExternalInput").ap()
    conv_w = nc.dram_tensor("conv_w", [DIN, 4], F32, kind="ExternalInput").ap()
    conv_b = nc.dram_tensor("conv_b", [DIN], F32, kind="ExternalInput").ap()
    w_x = nc.dram_tensor("W_x", [2 * NST + 1, DIN], F32, kind="ExternalInput").ap()
    w_dt = nc.dram_tensor("w_dt", [DIN], F32, kind="ExternalInput").ap()
    b_dt = nc.dram_tensor("b_dt", [DIN], F32, kind="ExternalInput").ap()
    a_log = nc.dram_tensor("A_log", [DIN, NST], F32, kind="ExternalInput").ap()
    d_par = nc.dram_tensor("D_param", [DIN], F32, kind="ExternalInput").ap()
    w_out = nc.dram_tensor("W_out", [DM, DIN], F32, kind="ExternalInput").ap()
    w1 = nc.dram_tensor("W1", [H, DM], F32, kind="ExternalInput").ap()
    b1 = nc.dram_tensor("b1", [H], F32, kind="ExternalInput").ap()
    w2 = nc.dram_tensor("W2", [DM, H], F32, kind="ExternalInput").ap()
    b2 = nc.dram_tensor("b2", [DM], F32, kind="ExternalInput").ap()
    scale_in = nc.dram_tensor("scale", [1, 1], F32, kind="ExternalInput").ap()
    out_hbm = nc.dram_tensor("out", [B_LOC, L, DM], F32, kind="ExternalOutput").ap()

    winT_scr = nc.dram_tensor("winT_scr", [32, NKD, P, P], BF16, kind="Internal").ap()
    win1s_scr = nc.dram_tensor("win1s_scr", [16, P, 4 * NKD * P], BF16, kind="Internal").ap()
    woutT_scr = nc.dram_tensor("woutT_scr", [NDT, P, DM], BF16, kind="Internal").ap()
    w1T_scr = nc.dram_tensor("w1T_scr", [NKD, P, H], BF16, kind="Internal").ap()
    w2T_scr = nc.dram_tensor("w2T_scr", [NKH, P, DM], BF16, kind="Internal").ap()
    y1_scr = nc.dram_tensor("y1_scr", [B_LOC, L, DM], BF16, kind="Internal").ap()
    bc_scr = nc.dram_tensor("bc_scr", [2, 2 * NST + 1, T], F32, kind="Internal").ap()

    ev = Ev(nc, known=known_events)

    ident = nc.alloc_sbuf_tensor("ident", [P, P], BF16)
    wxT_sb = nc.alloc_sbuf_tensor("wxT_sb", [P, NDT * 33], BF16)
    a_sb = nc.alloc_sbuf_tensor("a_sb", [P, NDT * NST], F32)
    wdt_sb = nc.alloc_sbuf_tensor("wdt_sb", [P, NDT], F32)
    bdt_sb = nc.alloc_sbuf_tensor("bdt_sb", [P, NDT], F32)
    dp_sb = nc.alloc_sbuf_tensor("dp_sb", [P, NDT], F32)
    cb_sb = nc.alloc_sbuf_tensor("cb_sb", [P, NDT], F32)
    cw_sb = nc.alloc_sbuf_tensor("cw_sb", [P, NDT * 4], F32)
    b1_sb = nc.alloc_sbuf_tensor("b1_sb", [P, NMH], F32)
    b2_sb = nc.alloc_sbuf_tensor("b2_sb", [1, DM], BF16)
    ones_sb = nc.alloc_sbuf_tensor("ones_sb", [1, P], BF16)
    onesT_sb = nc.alloc_sbuf_tensor("onesT_sb", [1, T], BF16)
    cb_row = nc.alloc_sbuf_tensor("cb_row", [1, DIN], BF16)
    scl_sb = nc.alloc_sbuf_tensor("scl_sb", [P, 1], F32)

    with contextlib.ExitStack() as st0:
        wrow = [st0.enter_context(nc.sbuf_tensor(f"p0_wrow{j}", [P, H], BF16)) for j in range(2)]
        stage = [st0.enter_context(nc.sbuf_tensor(f"p0_stage{j}", [P, NKH * P], BF16)) for j in range(2)]
        wx_nat = st0.enter_context(nc.sbuf_tensor("p0_wx", [2 * NST + 1, DIN], BF16))
        wsc = [st0.enter_context(nc.sbuf_tensor(f"p0_wsc{j}", [P, DM], BF16)) for j in range(4)]
        onesf = st0.enter_context(nc.sbuf_tensor("p0_onesf", [P, P], BF16))
        alog_sb = st0.enter_context(nc.sbuf_tensor("p0_alog", [P, NDT * NST], F32))
        ps_a = st0.enter_context(nc.psum_tensor("p0_psa", [P, 4 * P], BF16))
        ps_b = st0.enter_context(nc.psum_tensor("p0_psb", [P, 4 * P], BF16))

        with nc.Block() as blk:
            @blk.gpsimd
            def _(g):
                g.memset(onesf[:], 1.0)
                g.drain()
                g.affine_select(ident[:], onesf[:], [[-1, P]], OP.is_equal, 0.0,
                                base=0, channel_multiplier=1)
                g.memset(ones_sb[:], 1.0)
                g.memset(onesT_sb[:], 1.0)
                i = g.dma_start(out=cb_row[:], in_=conv_b.unsqueeze(0))
                ev.inc(i, "gw", 16)
                i = g.dma_start(out=wx_nat[:], in_=w_x)
                ev.inc(i, "gw", 16)
                i = g.dma_start(out=b2_sb[:], in_=b2.unsqueeze(0))
                ev.inc(i, "gw", 16)
                i = g.dma_start(out=scl_sb[:], in_=scale_in.broadcast_to([P, 1]))
                ev.inc(i, "gw", 16)
                i = g.drain()
                ev.inc(i, "gc", 1)
                ev.mark("p0_gp", "gc")
                ev.mark("p0_gw", "gw")

            @blk.sync
            def _(sy):
                i = sy.dma_start(out=a_sb[:].rearrange("p (k n) -> p k n", k=NDT),
                                 in_=a_log.rearrange("(k p) n -> p k n", k=NDT))
                ev.inc(i, "hd", 16)
                for src, dst in [(w_dt, wdt_sb), (b_dt, bdt_sb), (d_par, dp_sb),
                                 (conv_b, cb_sb)]:
                    i = sy.dma_start(out=dst[:],
                                     in_=src.rearrange("(k p) -> p k", k=NDT),
                                     allow_slow_non_contiguous=True)
                    ev.inc(i, "hd", 16)
                i = sy.dma_start(out=cw_sb[:].rearrange("p (k c) -> p k c", k=NDT),
                                 in_=conv_w.rearrange("(k p) c -> p k c", k=NDT))
                ev.inc(i, "hd", 16)
                i = sy.dma_start(out=b1_sb[:],
                                 in_=b1.rearrange("(k p) -> p k", k=NMH),
                                 allow_slow_non_contiguous=True)
                ev.inc(i, "hd", 16)
                ev.mark("p0_params", "hd")

        with nc.Block() as blk:
            @blk.scalar
            def _(s):
                ev.wait(s, "p0_params")
                i = s.activation(alog_sb[:], a_sb[:], AF.Exp)
                ev.inc(i, "ac", 1)
                ev.mark("p0_expA", "ac")

            @blk.vector
            def _(v):
                ev.wait(v, "p0_expA")
                i = v.tensor_scalar(a_sb[:], alog_sb[:], -1.0, None, OP.mult)
                ev.inc(i, "vc", 1)
                ev.mark("p0_negA", "vc")

        _prev_nr = {"ws": 16, "win": 32, "wout": 8, "w1": 32, "w2": 8}
        JOBS = [
            ("ws", w_in, 2048, DM, True),
            ("win", w_in, 2 * DIN, DM, False),
            ("wout", w_out, DM, DIN, False),
            ("w1", w1, H, DM, False),
            ("w2", w2, DM, H, False),
        ]

        def scr_write(sy, tag, r, k4, stg):
            if tag == "ws":
                return sy.dma_start(
                    out=win1s_scr[r][:, k4 * NKD * P:(k4 + 1) * NKD * P]
                    .rearrange("p (k c) -> p k c", k=NKD),
                    in_=stg[:, 0:DM].rearrange("p (k c) -> p k c", k=NKD))
            if tag == "win":
                return sy.dma_start(
                    out=winT_scr[r].rearrange("k p c -> p k c"),
                    in_=stg[:, 0:DM].rearrange("p (k c) -> p k c", k=NKD))
            if tag == "wout":
                return sy.dma_start(
                    out=woutT_scr[:, :, r * P:(r + 1) * P].rearrange("k p c -> p k c"),
                    in_=stg[:, 0:DIN].rearrange("p (k c) -> p k c", k=NDT))
            if tag == "w1":
                return sy.dma_start(
                    out=w1T_scr[:, :, r * P:(r + 1) * P].rearrange("k p c -> p k c"),
                    in_=stg[:, 0:DM].rearrange("p (k c) -> p k c", k=NKD))
            return sy.dma_start(
                out=w2T_scr[:, :, r * P:(r + 1) * P].rearrange("k p c -> p k c"),
                in_=stg[:, 0:H].rearrange("p (k c) -> p k c", k=NKH))

        with nc.Block() as blk:
            @blk.tensor
            def _(t):
                ev.wait(t, "p0_gp")
                ev.wait(t, "p0_gw")
                for kt in range(NDT):
                    pst = (ps_a, ps_b)[kt % 2]
                    if kt >= 2:
                        ev.wait(t, f"p0wx_ev{kt - 2}")
                    i = t.transpose(pst[0:P, 0:33],
                                    wx_nat[0:2 * NST + 1, kt * P:(kt + 1) * P],
                                    ident[0:2 * NST + 1, 0:2 * NST + 1])
                    ev.inc(i, "pe", 1)
                    ev.mark(f"p0wx_tr{kt}", "pe")
                g_idx = 0
                for tag, _sa, rows, cols, scaled in JOBS:
                    n_r, n_c = rows // P, cols // P
                    for r in range(n_r):
                        nk4 = 4 if scaled else 1
                        for k4 in range(nk4):
                            if scaled:
                                ev.wait(t, f"{tag}_sc{r}")
                            else:
                                ev.wait(t, f"{tag}_ld{r}")
                            for c in range(n_c):
                                q = g_idx // 4
                                pst = (ps_a, ps_b)[q % 2]
                                if g_idx % 4 == 0:
                                    if q >= 2:
                                        ev.wait(t, f"p0ev_{q - 2}")
                                    else:
                                        ev.wait(t, f"p0wx_ev{NDT - 2}")
                                        ev.wait(t, f"p0wx_ev{NDT - 1}")
                                src_t = wsc[k4] if scaled else wrow[r % 2]
                                i = t.transpose(pst[:, (g_idx % 4) * P:(g_idx % 4 + 1) * P],
                                                src_t[:, c * P:(c + 1) * P], ident[:])
                                ev.inc(i, "pe", 1)
                                if g_idx % 4 == 3:
                                    ev.mark(f"p0tr_{q}", "pe")
                                g_idx += 1
                        ev.mark(f"{tag}_tr{r}", "pe")

            @blk.gpsimd
            def _(g):
                for tag, src_ap, rows, cols, scaled in JOBS:
                    n_r = rows // P
                    for r in range(n_r):
                        if tag == "ws":
                            if r >= 2:
                                ev.wait(g, f"ws_sc{r - 2}")
                        else:
                            if r >= 2:
                                ev.wait(g, f"{tag}_tr{r - 2}")
                            else:
                                ptag = JOBS[[jj[0] for jj in JOBS].index(tag) - 1][0]
                                if ptag == "ws":
                                    ev.wait(g, f"ws_sc{15 - (1 - r)}")
                                else:
                                    ev.wait(g, f"{ptag}_tr{(_prev_nr[ptag]) - (2 - r)}")
                        i = g.dma_start(out=wrow[r % 2][:, 0:cols],
                                        in_=src_ap[r * P:(r + 1) * P, :])
                        ev.inc(i, f"gw{r % 2}", 16)
                        ev.mark(f"{tag}_ld{r}", f"gw{r % 2}")
                g.wait_ge(ev.sem("gw0"), ev.cnt["gw0"])
                g.wait_ge(ev.sem("gw1"), ev.cnt["gw1"])

            @blk.vector
            def _(v):
                ev.wait(v, "p0_params")
                ev.wait(v, "p0_negA")
                for r in range(16):
                    ev.wait(v, f"ws_ld{r}")
                    if r >= 1:
                        ev.wait(v, f"ws_tr{r - 1}")
                    for k4 in range(4):
                        i = v.tensor_scalar(wsc[k4][:], wrow[r % 2][:, 0:DM],
                                            cw_sb[:, r * 4 + k4:r * 4 + k4 + 1],
                                            None, OP.mult)
                    i = v.drain()
                    ev.inc(i, "vc", 1)
                    ev.mark(f"ws_sc{r}", "vc")

            @blk.scalar
            def _(s):
                for kt in range(NDT):
                    pst = (ps_a, ps_b)[kt % 2]
                    ev.wait(s, f"p0wx_tr{kt}")
                    i = s.copy(wxT_sb[:, kt * 33:(kt + 1) * 33], pst[0:P, 0:33])
                    ev.inc(i, "ac", 1)
                    ev.mark(f"p0wx_ev{kt}", "ac")
                g_idx = 0
                st_idx = 0
                for tag, _sa, rows, cols, scaled in JOBS:
                    n_r, n_c = rows // P, cols // P
                    for r in range(n_r):
                        nk4 = 4 if scaled else 1
                        for k4 in range(nk4):
                            if st_idx >= 2:
                                ev.wait(s, f"p0st_{st_idx - 2}")
                            for c in range(0, n_c, 4):
                                q = g_idx // 4
                                pst = (ps_a, ps_b)[q % 2]
                                ev.wait(s, f"p0tr_{q}")
                                i = s.copy(stage[st_idx % 2][:, c * P:(c + 4) * P], pst[:])
                                ev.inc(i, "ac", 1)
                                ev.mark(f"p0ev_{q}", "ac")
                                g_idx += 4
                            ev.mark(f"p0evs_{st_idx}", "ac")
                            st_idx += 1

            @blk.sync
            def _(sy):
                st_idx = 0
                for tag, _sa, rows, cols, scaled in JOBS:
                    n_r = rows // P
                    for r in range(n_r):
                        nk4 = 4 if scaled else 1
                        for k4 in range(nk4):
                            ev.wait(sy, f"p0evs_{st_idx}")
                            i = scr_write(sy, tag, r, k4, stage[st_idx % 2])
                            ev.inc(i, f"hd{st_idx % 2}", 16)
                            ev.mark(f"p0st_{st_idx}", f"hd{st_idx % 2}")
                            st_idx += 1
                        ev.mark(f"{tag}_st{r}a", "hd0")
                        ev.mark(f"{tag}_st{r}b", "hd1")
                    ev.mark(f"{tag}_alldoneA", "hd0")
                    ev.mark(f"{tag}_alldoneB", "hd1")
                sy.wait_ge(ev.sem("hd0"), ev.cnt["hd0"])
                sy.wait_ge(ev.sem("hd1"), ev.cnt["hd1"])

    nc.all_engine_barrier()

    stA = contextlib.ExitStack()
    tsb = lambda nm, sh, dt: stA.enter_context(nc.sbuf_tensor(nm, sh, dt))
    TE = 4 + T
    w1s_ring = [tsb(f"w1s_r{j}", [P, 4 * NKD * P], BF16) for j in range(2)]
    win_ring = [tsb(f"win_r{j}", [P, NKD * P], BF16) for j in range(2)]
    wout_ring = [tsb(f"wout_r{j}", [P, DM], BF16) for j in range(2)]
    x_bf = [tsb(f"x_bf{j}", [P, 2 * DM], BF16) for j in range(3)]
    xT = [tsb(f"xT{j}", [P, NKD * TE], BF16) for j in range(2)]
    xc = [tsb(f"xc{j}", [P, NDT * T], BF16) for j in range(2)]
    zsil = [tsb(f"zsil{j}", [P, NDT * T], BF16) for j in range(2)]
    vv = tsb("vv", [P, NDT * T], BF16)
    brep = [tsb(f"brep{j}", [P, NST * T], BF16) for j in range(2)]
    crep = [tsb(f"crep{j}", [P, NST * T], BF16) for j in range(2)]
    dtrep = [tsb(f"dtrep{j}", [P, T], F32) for j in range(2)]
    delta = [tsb(f"delta{j}", [P, T], F32) for j in range(8)]
    dA = [tsb(f"dA{j}", [P, NST * T], FP16) for j in range(2)]
    dbx = [tsb(f"dbx{j}", [P, NST * T], BF16) for j in range(2)]
    hc = tsb("hc", [P, NST * T], BF16)
    fixb = tsb("fixb", [P, NST], F32)
    scar = tsb("scar", [P, NDT * NST], F32)
    gtmp = tsb("gtmp", [P, T], F32)
    ybuf = [tsb(f"ybuf{j}", [P, NDT * T], BF16) for j in range(2)]
    y1pre = [tsb(f"y1pre{j}", [P, DM], F32) for j in range(2)]
    bcsb = tsb("bcsb", [2 * NST + 1, T], F32)
    lncol = tsb("lncol", [P, 16], F32)
    ps_tr = stA.enter_context(nc.psum_tensor("ps_tr", [P, P], BF16))
    ps_xz = [stA.enter_context(nc.psum_tensor(f"ps_xz{j}", [P, T], F32)) for j in range(2)]
    ps_xp = stA.enter_context(nc.psum_tensor("ps_xp", [2 * NST + 1, T], F32))
    ps_wo = [stA.enter_context(nc.psum_tensor(f"ps_wo{j}", [P, 512], F32)) for j in range(4)]

    def it_bk(i):
        return i // NCH, i % NCH

    def emit_wout_kt(t, i, kt):
        par = i % 2
        ev.wait(t, f"ydt_{i}_{kt}")
        if kt == 0 and i > 0:
            for grp in range(4):
                ev.wait(t, f"woev_{i - 1}_{grp}")
        jj = i * NDT + kt
        ev.wait(t, f"wold_{jj}")
        for grp in range(4):
            tM, nh = grp // 2, grp % 2
            jx = t.matmul(ps_wo[grp][:],
                          ybuf[par][:, kt * T + tM * P:kt * T + (tM + 1) * P],
                          wout_ring[jj % 2][:, nh * 512:(nh + 1) * 512],
                          start=(kt == 0), stop=(kt == NDT - 1))
        ev.inc(jx, "pe", 1)
        ev.mark(f"woK_{jj}", "pe")
        if kt == NDT - 1:
            ev.mark(f"wo_{i}", "pe")

    def emit_ln1_evac(v, i):
        par = i % 3
        v.drain()
        for grp in range(4):
            tM, nh = grp // 2, grp % 2
            ev.wait(v, f"wo_{i}")
            jx = v.tensor_tensor(y1pre[tM][:, nh * 512:(nh + 1) * 512],
                                 ps_wo[grp][:],
                                 x_bf[par][:, tM * DM + nh * 512:tM * DM + (nh + 1) * 512],
                                 OP.add)
            jx = ev.inc(jx, "vc", 1)
            ev.mark(f"woev_{i}_{grp}", "vc")
        ev.mark(f"ln_add_{i}", "vc")
        for tM in range(2):
            ev.mark(f"lnadd_{i}_{tM}", "vc")
        v.drain()

    def emit_ln1_norm(v, i):
        par = i % 3
        v.drain()
        if i >= 2:
            ev.wait(v, f"y1st_{i - 2}")
        for tM in range(2):
            ev.wait(v, f"rstd_{i}_{tM}")
            jx = v.tensor_scalar(x_bf[par][:, tM * DM:(tM + 1) * DM], y1pre[tM][:],
                                 lncol[:, 8 * tM + 2:8 * tM + 3],
                                 lncol[:, 8 * tM + 7:8 * tM + 8],
                                 OP.subtract, OP.mult)
        jx = ev.inc(jx, "vc", 1)
        ev.mark(f"ln_norm_{i}", "vc")
        v.drain()

    def emit_lnstat(s, i):
        c = lncol
        for tM in range(2):
            ev.wait(s, f"lnadd_{i}_{tM}")
            dmp = ybuf[i % 2][:, 0:2 * DM].bitcast(F32)
            jx = s.activation(dmp, y1pre[tM][:], AF.Square,
                              accum_out=c[:, 8 * tM + 0:8 * tM + 1])
            s.drain()
            jx = s.activation(dmp, y1pre[tM][:], AF.Copy,
                              accum_out=c[:, 8 * tM + 1:8 * tM + 2])
            s.drain()
        for tM in range(2):
            s.activation(c[:, 8 * tM + 2:8 * tM + 3], c[:, 8 * tM + 1:8 * tM + 2],
                         AF.Copy, scale=1.0 / DM)
        s.drain()
        for tM in range(2):
            s.activation(c[:, 8 * tM + 3:8 * tM + 4], c[:, 8 * tM + 2:8 * tM + 3],
                         AF.Square)
        s.drain()
        for tM in range(2):
            s.activation(c[:, 8 * tM + 4:8 * tM + 5], c[:, 8 * tM + 3:8 * tM + 4],
                         AF.Copy, scale=-1.0, bias=EPS)
        s.drain()
        for tM in range(2):
            s.activation(c[:, 8 * tM + 6:8 * tM + 7], c[:, 8 * tM + 0:8 * tM + 1],
                         AF.Ln, scale=1.0 / DM, bias=c[:, 8 * tM + 4:8 * tM + 5])
        s.drain()
        for tM in range(2):
            jx = s.activation(c[:, 8 * tM + 7:8 * tM + 8], c[:, 8 * tM + 6:8 * tM + 7],
                              AF.Exp, scale=-0.5)
        ev.inc(jx, "ac", 1)
        for tM in range(2):
            ev.mark(f"rstd_{i}_{tM}", "ac")

    if not do_a:
        n_iters = 0
    with nc.Block() as blk:
        @blk.sync
        def _(sy):
            def wout_slabs(i):
                for kt in range(NDT):
                    jj = i * NDT + kt
                    if jj >= 2:
                        ev.wait(sy, f"woK_{jj - 2}")
                    j = sy.dma_start(out=wout_ring[jj % 2][:], in_=woutT_scr[kt])
                    ev.inc(j, f"wor{jj % 2}", 16)
                    ev.mark(f"wold_{jj}", f"wor{jj % 2}")

            for e in ["ws_alldoneA", "ws_alldoneB", "win_alldoneA", "win_alldoneB",
                      "wout_alldoneA", "wout_alldoneB"]:
                ev.wait(sy, e)
            for i in range(n_iters):
                for m in range(NDT):
                    jj = i * NDT + m
                    if jj >= 2:
                        ev.wait(sy, f"xzMx_{jj - 2}")
                    j = sy.dma_start(out=w1s_ring[jj % 2][:], in_=win1s_scr[m])
                    ev.inc(j, f"w1r{jj % 2}", 16)
                    ev.mark(f"xsl_{jj}", f"w1r{jj % 2}")
                for m in range(NDT):
                    jj = i * NDT + m
                    if jj >= 2:
                        ev.wait(sy, f"xzMz_{jj - 2}")
                    j = sy.dma_start(out=win_ring[jj % 2][:].rearrange("p (k c) -> p k c", k=NKD),
                                     in_=winT_scr[NDT + m].rearrange("k p c -> p k c"))
                    ev.inc(j, f"wzr{jj % 2}", 16)
                    ev.mark(f"xszl_{jj}", f"wzr{jj % 2}")
                if i > 0:
                    wout_slabs(i - 1)
            wout_slabs(n_iters - 1)

        def emit_y1st(g, i):
            b, k = it_bk(i)
            ev.wait(g, f"ln_norm_{i}")
            j = g.dma_start(out=y1_scr[b, k * T:(k + 1) * T, :]
                            .rearrange("(tt p) d -> p tt d", tt=2),
                            in_=x_bf[i % 3][:].rearrange("p (tt d) -> p tt d", tt=2))
            ev.inc(j, f"gy{i % 2}", 16)
            ev.mark(f"y1st_{i}", f"gy{i % 2}")

        def emit_xcast(g, i):
            b, k = it_bk(i)
            if i >= 3:
                ev.wait(g, f"y1st_{i - 3}")
            j = g.dma_start(out=x_bf[i % 3][:].rearrange("p (tt d) -> p tt d", tt=2),
                            in_=x_in[b, k * T:(k + 1) * T, :]
                            .rearrange("(tt p) d -> p tt d", tt=2))
            ev.inc(j, f"gx{i % 2}", 16)
            ev.mark(f"xld_{i}", f"gx{i % 2}")

        @blk.gpsimd
        def _(g):
            ev.wait(g, "p0_params")
            emit_xcast(g, 0)
            for i in range(n_iters):
                b, k = it_bk(i)
                par = i % 2
                ev.wait(g, f"xpev_{i}")
                j = g.dma_start(out=bc_scr[par], in_=bcsb[:])
                ev.inc(j, "gb", 16)
                ev.mark(f"bounce_{i}", "gb")
                g.wait_ge(ev.sem("gb"), ev.cnt["gb"])
                if i >= 2:
                    ev.wait(g, f"dloop_{i - 2}")
                j = g.dma_start(out=brep[par][:].rearrange("p (n t) -> p n t", n=NST),
                                in_=bc_scr[par][0:NST].unsqueeze(0).broadcast_to([P, NST, T]))
                ev.inc(j, "gr", 16)
                j = g.dma_start(out=crep[par][:].rearrange("p (n t) -> p n t", n=NST),
                                in_=bc_scr[par][NST:2 * NST].unsqueeze(0).broadcast_to([P, NST, T]))
                ev.inc(j, "gr", 16)
                j = g.dma_start(out=dtrep[par][:],
                                in_=bc_scr[par][2 * NST:2 * NST + 1].broadcast_to([P, T]))
                ev.inc(j, "gr", 16)
                ev.mark(f"rep_{i}", "gr")
                def vhalf(h):
                    ev.wait(g, f"deltaH_{i}_{h}")
                    ev.wait(g, f"siluxc_{i}_{h * 8 + 7}")
                    if i >= 2 and h == 0:
                        ev.wait(g, f"dloop_{i - 2}")
                    for dt in range(h * 8, h * 8 + 8):
                        j = g.tensor_tensor(vv[:, dt * T:(dt + 1) * T],
                                            delta[dt % 8][:],
                                            xc[par][:, dt * T:(dt + 1) * T], OP.mult)
                    j = g.drain()
                    ev.inc(j, "gc", 1)
                    ev.mark(f"vH_{i}_{h}", "gc")
                def emit_dbx(dt):
                    jd = i * NDT + dt
                    if jd >= 2:
                        ev.wait(g, f"scan_{jd - 2}")
                    ev.wait(g, f"dA_{jd}")
                    j = g.tensor_tensor(
                        dbx[jd % 2][:].rearrange("p (n t) -> p n t", n=NST),
                        vv[:, dt * T:(dt + 1) * T].unsqueeze(1)
                        .broadcast_to([P, NST, T]),
                        brep[par][:].rearrange("p (n t) -> p n t", n=NST),
                        OP.mult)
                    g.drain()
                    if k > 0:
                        ev.wait(g, f"scc_{jd - 1}")
                        j = g.tensor_tensor(
                            fixb[:],
                            dA[jd % 2][:].rearrange("p (n t) -> p n t", n=NST)[:, :, 0],
                            scar[:, dt * NST:(dt + 1) * NST],
                            OP.mult)
                        g.drain()
                        j = g.tensor_tensor(
                            dbx[jd % 2][:].rearrange("p (n t) -> p n t", n=NST)[:, :, 0],
                            dbx[jd % 2][:].rearrange("p (n t) -> p n t", n=NST)[:, :, 0],
                            fixb[:], OP.add)
                    j = g.memset(dA[jd % 2][:]
                                 .rearrange("p (n t) -> p n t", n=NST)[:, :, 0:1], 0.0)
                    j = g.drain()
                    ev.inc(j, "gc", 1)
                    ev.mark(f"dbxr_{jd}", "gc")

                if i + 1 < n_iters:
                    emit_xcast(g, i + 1)
                vhalf(0)
                emit_dbx(0)
                emit_dbx(1)
                if i > 0:
                    emit_y1st(g, i - 1)
                for dt in range(2, 8):
                    emit_dbx(dt)
                vhalf(1)
                for dt in range(8, NDT):
                    emit_dbx(dt)
            if n_iters > 0:
                emit_y1st(g, n_iters - 1)
            g.wait_ge(ev.sem("gy0"), ev.cnt["gy0"])
            g.wait_ge(ev.sem("gy1"), ev.cnt["gy1"])


        @blk.tensor
        def _(t):
            ev.wait(t, "p0_gp")
            ev.wait(t, "p0_params")
            ev.wait(t, "p0_gw")
            for i in range(n_iters):
                par = i % 2
                ev.wait(t, f"xld_{i}")
                for j2 in range(16):
                    tt, dd = j2 // NKD, j2 % NKD
                    jj = i * 16 + j2
                    if jj >= 1:
                        ev.wait(t, f"trev_{(jj - 1) // 16}_{(jj - 1) % 16}")
                    jx = t.transpose(ps_tr[:],
                                     x_bf[i % 3][:, tt * DM + dd * P:tt * DM + (dd + 1) * P],
                                     ident[:])
                    ev.inc(jx, "pe", 1)
                    ev.mark(f"tr_{i}_{j2}", "pe")
                ev.wait(t, f"trev_{i}_15")
                for m in range(NDT):
                    jj = i * NDT + m
                    ev.wait(t, f"xsl_{jj}")
                    pst = ps_xz[m % 2]
                    if i * 32 + m >= 2:
                        j3 = i * 32 + m - 2
                        ev.wait(t, f"xzev_{j3 // 32}_{j3 % 32}")
                    for kt in range(NKD):
                        for k4 in range(4):
                            jx = t.matmul(pst[:],
                                          w1s_ring[jj % 2][:, (k4 * NKD + kt) * P:(k4 * NKD + kt + 1) * P],
                                          xT[par][:, kt * TE + 1 + k4:kt * TE + 1 + k4 + T],
                                          start=(kt == 0 and k4 == 0), stop=False)
                    jx = t.matmul(pst[:], cb_row[0:1, m * P:(m + 1) * P],
                                  onesT_sb[0:1, 0:T], start=False, stop=True)
                    ev.inc(jx, "pe", 1)
                    ev.mark(f"xzMx_{jj}", "pe")
                    ev.mark(f"xzMl_{i}_{m}", "pe")
                for m in range(NDT):
                    jj = i * NDT + m
                    ev.wait(t, f"xszl_{jj}")
                    pst = ps_xz[m % 2]
                    j3 = i * 32 + NDT + m - 2
                    ev.wait(t, f"xzev_{j3 // 32}_{j3 % 32}")
                    for kt in range(NKD):
                        jx = t.matmul(pst[:], win_ring[jj % 2][:, kt * P:(kt + 1) * P],
                                      xT[par][:, kt * TE + 4:kt * TE + 4 + T],
                                      start=(kt == 0), stop=(kt == NKD - 1))
                    ev.inc(jx, "pe", 1)
                    ev.mark(f"xzMz_{jj}", "pe")
                    ev.mark(f"xzMl_{i}_{NDT + m}", "pe")
                ev.wait(t, f"siluxc_{i}_{NDT - 1}")
                if i > 0:
                    ev.wait(t, f"xpev_{i - 1}")
                for kt in range(NDT):
                    jx = t.matmul(ps_xp[:], wxT_sb[:, kt * 33:(kt + 1) * 33],
                                  xc[par][:, kt * T:(kt + 1) * T],
                                  start=(kt == 0), stop=(kt == NDT - 1))
                ev.inc(jx, "pe", 1)
                ev.mark(f"xp_{i}", "pe")
                if i > 0:
                    for kt in range(NDT):
                        emit_wout_kt(t, i - 1, kt)
            if n_iters > 0:
                for kt in range(NDT):
                    emit_wout_kt(t, n_iters - 1, kt)

        @blk.scalar
        def _(s):
            ev.wait(s, "p0_params")
            ev.wait(s, "p0_negA")
            ev.wait(s, "p0_gw")

            def prep_tails(i):
                b, k = it_bk(i)
                par = i % 2
                s.drain()
                if i >= 2:
                    ev.wait(s, f"xzMz_{(i - 1) * NDT + NDT - 1}")
                for dd in range(NKD):
                    if k == 0:
                        s.memzero(xT[par][:, dd * TE:dd * TE + 4])
                    else:
                        s.copy(xT[par][:, dd * TE + 1:dd * TE + 4],
                               xT[1 - par][:, dd * TE + 4 + T - 3:dd * TE + 4 + T])
                s.drain()

            def prep_trev(i, j2):
                tt, dd = j2 // NKD, j2 % NKD
                par = i % 2
                ev.wait(s, f"tr_{i}_{j2}")
                jx = s.copy(xT[par][:, dd * TE + 4 + tt * P:dd * TE + 4 + (tt + 1) * P],
                            ps_tr[:])
                jx = ev.inc(jx, "ac", 1)
                ev.mark(f"trev_{i}_{j2}", "ac")

            def prep_evac(i, m):
                b, k = it_bk(i)
                par = i % 2
                ev.wait(s, f"xzMl_{i}_{m}")
                pst = ps_xz[m % 2]
                if m < NDT:
                    dt = m
                    if i >= 2 and dt == 0:
                        ev.wait(s, f"xp_{i - 2}")
                        ev.wait(s, f"dloop_{i - 2}")
                    jx = s.activation(xc[par][:, dt * T:(dt + 1) * T], pst[:], AF.Silu)
                    ev.inc(jx, "ac", 1)
                    ev.mark(f"siluxc_{i}_{dt}", "ac")
                    ev.mark(f"xzev_{i}_{m}", "ac")
                else:
                    dt = m - NDT
                    if i >= 2 and dt == 0:
                        ev.wait(s, f"dloop_{i - 2}")
                    jx = s.activation(zsil[par][:, dt * T:(dt + 1) * T], pst[:], AF.Silu)
                    ev.inc(jx, "ac", 1)
                    ev.mark(f"zsil_{i}_{dt}", "ac")
                    ev.mark(f"xzev_{i}_{m}", "ac")

            def prep_xpev(i):
                ev.wait(s, f"xp_{i}")
                if i > 0:
                    ev.wait(s, f"bounce_{i - 1}")
                jx = s.copy(bcsb[:], ps_xp[:])
                jx = ev.inc(jx, "ac", 1)
                ev.mark(f"xpev_{i}", "ac")

            def prep_delta_q(i, q):
                par = i % 2
                ev.wait(s, f"rep_{i}")
                if q == 0 and i > 0:
                    ev.wait(s, f"vH_{i - 1}_{1}")
                if q == 2:
                    ev.wait(s, f"vH_{i}_{0}")
                s.drain()
                for dt in range(q * 4, q * 4 + 4):
                    s.activation(delta[dt % 8][:], dtrep[par][:], AF.Exp,
                                 bias=bdt_sb[:, dt:dt + 1], scale=wdt_sb[:, dt:dt + 1])
                s.drain()
                for dt in range(q * 4, q * 4 + 4):
                    jx = s.activation(delta[dt % 8][:], delta[dt % 8][:], AF.Ln, bias=1.0)
                s.drain()
                if q % 2 == 1:
                    jx = ev.inc(jx, "ac", 1)
                    ev.mark(f"deltaH_{i}_{q // 2}", "ac")

            def emit_dA(i, dt):
                jd = i * NDT + dt
                da_t = dA[jd % 2]
                if jd >= 2:
                    ev.wait(s, f"scan_{jd - 2}")
                for n in range(NST):
                    jx = s.activation(da_t[:, n * T:(n + 1) * T], delta[dt % 8][:],
                                      AF.Exp,
                                      scale=a_sb[:, dt * NST + n:dt * NST + n + 1])
                jx = ev.inc(jx, "ac", 1)
                ev.mark(f"dA_{jd}", "ac")

            if n_iters > 0:
                prep_tails(0)
                for j2 in range(16):
                    prep_trev(0, j2)
                for m in range(32):
                    prep_evac(0, m)
                prep_xpev(0)
                prep_delta_q(0, 0)
                prep_delta_q(0, 1)

            for i in range(n_iters):
                nxt = i + 1 if i + 1 < n_iters else None
                for dt in range(NDT):
                    emit_dA(i, dt)
                    if dt == 0 and i > 0:
                        emit_lnstat(s, i - 1)
                    if dt == 4:
                        prep_delta_q(i, 2)
                    if dt == 7:
                        prep_delta_q(i, 3)
                    if nxt is not None:
                        if dt == 0:
                            prep_tails(nxt)
                        if 2 <= dt < 6:
                            for j2 in range(4 * (dt - 2), 4 * (dt - 2) + 4):
                                prep_trev(nxt, j2)
                        elif 6 <= dt <= 12:
                            sched = [5, 5, 5, 5, 4, 4, 4]
                            m0 = sum(sched[:dt - 6])
                            for mm in range(m0, m0 + sched[dt - 6]):
                                prep_evac(nxt, mm)
                        if dt == 13:
                            prep_xpev(nxt)
                if nxt is not None:
                    prep_delta_q(nxt, 0)
                    prep_delta_q(nxt, 1)
            if n_iters > 0:
                emit_lnstat(s, n_iters - 1)

        @blk.vector
        def _(v):
            ev.wait(v, "p0_params")
            for i in range(n_iters):
                b, k = it_bk(i)
                par = i % 2
                if i > 0:
                    emit_ln1_evac(v, i - 1)
                    emit_ln1_norm(v, i - 1)
                for dt in range(NDT):
                    if dt % 8 == 0:
                        ev.wait(v, f"zsil_{i}_{dt // 8 * 8 + 7}")
                    jd = i * NDT + dt
                    ev.wait(v, f"dbxr_{jd}")
                    HT = NST * T // 2
                    jx = v.tensor_tensor_scan(hc[:, 0:HT], dA[jd % 2][:, 0:HT],
                                              dbx[jd % 2][:, 0:HT], 0.0, OP.mult, OP.add)
                    jx = v.tensor_tensor_scan(hc[:, HT:], dA[jd % 2][:, HT:],
                                              dbx[jd % 2][:, HT:], 0.0, OP.mult, OP.add)
                    jx = ev.inc(jx, "vc", 1)
                    ev.mark(f"scan_{jd}", "vc")
                    v.drain()
                    jx = v.tensor_copy(
                        scar[:, dt * NST:(dt + 1) * NST],
                        hc[:].rearrange("p (n t) -> p n t", n=NST)[:, :, T - 1])
                    jx = ev.inc(jx, "vc", 1)
                    ev.mark(f"scc_{jd}", "vc")
                    v.drain()
                    jx = v.tensor_tensor(hc[:], hc[:], crep[par][:], OP.mult)
                    v.drain()
                    jx = v.tensor_add(hc[:, 0:8 * T], hc[:, 0:8 * T], hc[:, 8 * T:16 * T])
                    v.drain()
                    jx = v.tensor_add(hc[:, 0:4 * T], hc[:, 0:4 * T], hc[:, 4 * T:8 * T])
                    v.drain()
                    jx = v.tensor_add(hc[:, 0:2 * T], hc[:, 0:2 * T], hc[:, 2 * T:4 * T])
                    v.drain()
                    jx = v.tensor_add(hc[:, 0:T], hc[:, 0:T], hc[:, T:2 * T])
                    v.drain()
                    jx = v.scalar_tensor_tensor(gtmp[:], xc[par][:, dt * T:(dt + 1) * T],
                                                dp_sb[:, dt:dt + 1], hc[:, 0:T],
                                                OP.mult, OP.add)
                    v.drain()
                    jx = v.tensor_tensor(ybuf[par][:, dt * T:(dt + 1) * T], gtmp[:],
                                         zsil[par][:, dt * T:(dt + 1) * T], OP.mult)
                    jx = ev.inc(jx, "vc", 1)
                    ev.mark(f"ydt_{i}_{dt}", "vc")
                    v.drain()
                ev.mark(f"dloop_{i}", "vc")
            if n_iters > 0:
                emit_ln1_evac(v, n_iters - 1)
                emit_ln1_norm(v, n_iters - 1)

    stA.close()
    nc.all_engine_barrier()
    if not do_b:
        return nc, ev.events

    TB = 512
    NCB = L // TB
    nB = B_LOC * NCB
    stB = contextlib.ExitStack()
    tsb2 = lambda nm, sh, dt: stB.enter_context(nc.sbuf_tensor(nm, sh, dt))
    w1t_sb = tsb2("w1t_sb", [P, NKD * H], BF16)
    w2_ring = [tsb2(f"w2_r{j}", [P, DM], BF16) for j in range(4)]
    y1cb = [tsb2(f"y1cb{j}", [P, 4 * DM], BF16) for j in range(2)]
    y1T = [tsb2(f"y1T{j}", [P, NKD * TB], BF16) for j in range(2)]
    h_sb = tsb2("h_sb", [P, NKH * TB], BF16)
    ltT = tsb2("ltT", [P, NKD * NST], BF16)
    ltok = tsb2("ltok", [NST, DM], BF16)
    e1z = tsb2("e1z", [4, H], BF16)
    e2z = tsb2("e2z", [4, 4 * DM], BF16)
    su_sb = tsb2("su_sb", [4, TB], BF16)
    su2_sb = tsb2("su2_sb", [4, 4 * TB], BF16)
    y2pre = [tsb2(f"y2pre{j}", [P, DM], F32) for j in range(2)]
    outsb = [tsb2(f"outsb{j}", [P, DM], F32) for j in range(2)]
    ln2col = tsb2("ln2col", [P, 16], F32)
    lnjB = tsb2("lnjB", [P, DM], F32)
    ps_trB = stB.enter_context(nc.psum_tensor("ps_trB", [P, P], BF16))
    ps_h = [stB.enter_context(nc.psum_tensor(f"ps_h{j}", [P, TB], F32)) for j in range(2)]
    ps_ut = stB.enter_context(nc.psum_tensor("ps_ut", [4, TB], F32))
    ps_o = [stB.enter_context(nc.psum_tensor(f"ps_oB{j}", [P, 512], F32)) for j in range(4)]

    def itB(i):
        return i // NCB, i % NCB

    with nc.Block() as blk:
        @blk.sync
        def _(sy):
            for e in ["w1_alldoneA", "w1_alldoneB", "w2_alldoneA", "w2_alldoneB"]:
                ev.wait(sy, e)
            def w1chunk(kt):
                i = sy.dma_start(out=w1t_sb[:, kt * H:(kt + 1) * H], in_=w1T_scr[kt])
                ev.inc(i, "hd", 16)

            def w2slab(jj):
                kt = jj % NKH
                if jj >= 4:
                    ev.wait(sy, f"wo2K_{jj - 4}")
                j = sy.dma_start(out=w2_ring[jj % 4][:], in_=w2T_scr[kt])
                ev.inc(j, f"w2r{jj % 4}", 16)
                ev.mark(f"w2ld_{jj}", f"w2r{jj % 4}")

            w1chunk(0)
            w1chunk(1)
            w2slab(0)
            w2slab(1)
            for kt in range(2, NKD):
                w1chunk(kt)
            ev.mark("bw_ld", "hd")
            w2slab(2)
            w2slab(3)
            def emit_outst(i2):
                b, kc = itB(i2)
                for tM in range(4):
                    ev.wait(sy, f"outn_{i2}_{tM}")
                    j = sy.dma_start(
                        out=out_hbm[b, kc * TB + tM * P:kc * TB + (tM + 1) * P, :],
                        in_=outsb[tM % 2][:])
                    ev.inc(j, f"out{tM % 2}", 16)
                    ev.mark(f"outst_{i2}_{tM}", f"out{tM % 2}")

            for i2 in range(nB):
                if i2 > 0:
                    emit_outst(i2 - 1)
                for ps in range(2):
                    for kt in range(NKH):
                        jj = (i2 * 2 + ps) * NKH + kt
                        if jj >= 4:
                            w2slab(jj)
            emit_outst(nB - 1)
            sy.wait_ge(ev.sem("hd"), ev.cnt["hd"])
            sy.wait_ge(ev.sem("out0"), ev.cnt["out0"])
            sy.wait_ge(ev.sem("out1"), ev.cnt["out1"])

        @blk.gpsimd
        def _(g):
            for i2 in range(nB):
                b, kc = itB(i2)
                par = i2 % 2
                if kc == 0:
                    j = g.dma_start(out=ltok[:], in_=y1_scr[b, 1:NST + 1, :])
                    ev.inc(j, "gyt", 16)
                    ev.mark(f"ltok_{b}", "gyt")
                    if i2 > 0:
                        ev.wait(g, f"wo2_{i2 - 1}_7")
                    g.memset(e1z[:], 0.0)
                    g.memset(e2z[:], 0.0)
                    j = g.drain()
                    ev.inc(j, "gc", 1)
                    ev.mark(f"e2pm_{b}", "gc")
                    for i4 in range(4):
                        j = g.dma_start(out=e1z[i4:i4 + 1, i4 * DM:(i4 + 1) * DM],
                                        in_=y1_scr[b, 1 + 2 * i4:2 + 2 * i4, :])
                        ev.inc(j, "ge2", 16)
                        for plo in range(4):
                            j = g.dma_start(
                                out=e2z[i4:i4 + 1, plo * DM + i4 * T:plo * DM + (i4 + 1) * T],
                                in_=y1_scr[b, 9 + 2 * i4:10 + 2 * i4, plo:DM:4],
                                allow_slow_non_contiguous=True)
                            ev.inc(j, "ge2", 16)
                    ev.mark(f"e2p_{b}", "ge2")
                if i2 >= 2:
                    ev.wait(g, f"y2add_{i2 - 2}_3")
                j = g.dma_start(out=y1cb[par][:].rearrange("p (tt d) -> p tt d", tt=4),
                                in_=y1_scr[b, kc * TB:(kc + 1) * TB, :]
                                .rearrange("(tt p) d -> p tt d", tt=4))
                ev.inc(j, f"gyl{par}", 16)
                ev.mark(f"y1ld_{i2}", f"gyl{par}")

        @blk.tensor
        def _(t):
            for i2 in range(nB):
                b, kc = itB(i2)
                par = i2 % 2
                if kc == 0:
                    ev.wait(t, f"ltok_{b}")
                    for dd in range(NKD):
                        if b > 0 or dd >= 1:
                            pj = (b * NKD + dd - 1)
                            ev.wait(t, f"ltT_ev_{pj // NKD}_{pj % NKD}" if dd >= 1
                                    else f"y1T_ev_{i2 - 1}_31")
                        jx = t.transpose(ps_trB[0:P, 0:NST],
                                         ltok[0:NST, dd * P:(dd + 1) * P],
                                         ident[0:NST, 0:NST])
                        ev.inc(jx, "pe", 1)
                        ev.mark(f"ltT_tr_{b}_{dd}", "pe")
                ev.wait(t, f"y1ld_{i2}")
                for j2 in range(32):
                    tt, dd = j2 // NKD, j2 % NKD
                    jj = i2 * 32 + j2
                    if kc == 0 and j2 == 0:
                        ev.wait(t, f"ltT_ev_{b}_{NKD - 1}")
                    elif jj >= 1:
                        ev.wait(t, f"y1T_ev_{(jj - 1) // 32}_{(jj - 1) % 32}")
                    jx = t.transpose(ps_trB[:],
                                     y1cb[par][:, tt * DM + dd * P:tt * DM + (dd + 1) * P],
                                     ident[:])
                    ev.inc(jx, "pe", 1)
                    ev.mark(f"y1T_tr_{i2}_{j2}", "pe")
                ev.wait(t, f"y1T_ev_{i2}_31")
                ev.wait(t, f"ltTev_{b}")
                if i2 > 0:
                    ev.wait(t, f"su_{i2 - 1}")
                for kt in range(NKD):
                    jx = t.matmul(ps_ut[:], ltT[:, kt * NST + 1:kt * NST + 8:2],
                                  y1T[par][:, kt * TB:(kt + 1) * TB],
                                  start=(kt == 0), stop=(kt == NKD - 1))
                jx = ev.inc(jx, "pe", 1)
                ev.mark(f"upe_{i2}", "pe")
                ev.wait(t, f"su_{i2}")
                ev.wait(t, f"e2p_{b}")
                ev.wait(t, f"e2pm_{b}")
                if i2 == 0:
                    ev.wait(t, "bw_ld")
                for m in range(NMH):
                    pst = ps_h[m % 2]
                    jj = i2 * NMH + m - 2
                    if jj >= 0:
                        ev.wait(t, f"hev_{jj // NMH}_{jj % NMH}")
                    for kt in range(NKD):
                        jx = t.matmul(pst[:], w1t_sb[:, kt * H + m * P:kt * H + (m + 1) * P],
                                      y1T[par][:, kt * TB:(kt + 1) * TB],
                                      start=(kt == 0), stop=False)
                    jx = t.matmul(pst[:], e1z[0:4, m * P:(m + 1) * P],
                                  su_sb[0:4, :], start=False, stop=True)
                    ev.inc(jx, "pe", 1)
                    ev.mark(f"hpe_{i2}_{m}", "pe")
                for plo in range(4):
                    ev.wait(t, f"hev_{i2}_{plo * 8 + 7}")
                    jj = i2 * 4 + plo - 1
                    if jj >= 0:
                        ev.wait(t, f"su2_{jj // 4}_{jj % 4}")
                    for qt in range(NKD):
                        jx = t.matmul(ps_ut[:], ltT[:, qt * NST + 9:qt * NST + 16:2],
                                      h_sb[:, (plo * NKD + qt) * TB:(plo * NKD + qt + 1) * TB],
                                      start=(qt == 0), stop=(qt == NKD - 1))
                    jx = ev.inc(jx, "pe", 1)
                    ev.mark(f"t2pe_{i2}_{plo}", "pe")
                ev.wait(t, f"hev_{i2}_{NMH - 1}")
                for plo in range(4):
                    ev.wait(t, f"su2_{i2}_{plo}")
                for ps in range(2):
                    for kt in range(NKH):
                        jj = (i2 * 2 + ps) * NKH + kt
                        ev.wait(t, f"w2ld_{jj}")
                        for gq in range(4):
                            tM = ps * 2 + gq // 2
                            nh = gq % 2
                            if kt == 0:
                                pj = i2 * 8 + ps * 4 + gq - 4
                                if pj >= 0:
                                    ev.wait(t, f"o2ev_{pj // 8}_{pj % 8}")
                            jx = t.matmul(
                                ps_o[gq][:],
                                h_sb[:, kt * TB + tM * P:kt * TB + (tM + 1) * P],
                                w2_ring[jj % 4][:, nh * 512:(nh + 1) * 512],
                                start=(kt == 0), stop=False)
                        ev.inc(jx, "pe", 1)
                        ev.mark(f"wo2K_{jj}", "pe")
                    for gq in range(4):
                        tM = ps * 2 + gq // 2
                        nh = gq % 2
                        grp = ps * 4 + gq
                        for plo in range(4):
                            jx = t.matmul(ps_o[gq][:],
                                          su2_sb[0:4, plo * TB + tM * P:plo * TB + (tM + 1) * P],
                                          e2z[0:4, plo * DM + nh * 512:plo * DM + (nh + 1) * 512],
                                          start=False, stop=False)
                        jx = t.matmul(ps_o[gq][:], ones_sb[0:1, :],
                                      b2_sb[0:1, nh * 512:(nh + 1) * 512],
                                      start=False, stop=True)
                        ev.inc(jx, "pe", 1)
                        ev.mark(f"wo2_{i2}_{grp}", "pe")

        @blk.scalar
        def _(s):
            for i2 in range(nB):
                b, kc = itB(i2)
                par = i2 % 2
                if kc == 0:
                    for dd in range(NKD):
                        ev.wait(s, f"ltT_tr_{b}_{dd}")
                        jx = s.copy(ltT[:, dd * NST:(dd + 1) * NST], ps_trB[0:P, 0:NST])
                        jx = ev.inc(jx, "ac", 1)
                        ev.mark(f"ltT_ev_{b}_{dd}", "ac")
                    ev.mark(f"ltTev_{b}", "ac")
                for j2 in range(32):
                    tt, dd = j2 // NKD, j2 % NKD
                    ev.wait(s, f"y1T_tr_{i2}_{j2}")
                    jx = s.copy(y1T[par][:, dd * TB + tt * P:dd * TB + (tt + 1) * P],
                                ps_trB[:])
                    jx = ev.inc(jx, "ac", 1)
                    ev.mark(f"y1T_ev_{i2}_{j2}", "ac")
                for m in range(NMH):
                    pst = ps_h[m % 2]
                    ev.wait(s, f"hpe_{i2}_{m}")
                    if i2 >= 1 and m == 0:
                        ev.wait(s, f"wo2_{i2 - 1}_7")
                    jx = s.activation(h_sb[:, m * TB:(m + 1) * TB], pst[:], AF.Gelu,
                                      bias=b1_sb[:, m:m + 1])
                    jx = ev.inc(jx, "ac", 1)
                    ev.mark(f"hev_{i2}_{m}", "ac")
                c = ln2col
                for tM in range(4):
                    sl = 8 * (tM % 2)
                    ev.wait(s, f"y2add_{i2}_{tM}")
                    if tM >= 2:
                        ev.wait(s, f"outn_{i2}_{tM - 2}")
                    dmp = lnjB[:]
                    jx = s.activation(dmp, y2pre[tM % 2][:], AF.Square,
                                      accum_out=c[:, sl + 0:sl + 1])
                    s.drain()
                    jx = s.activation(dmp, y2pre[tM % 2][:], AF.Copy,
                                      accum_out=c[:, sl + 1:sl + 2])
                    s.drain()
                    s.activation(c[:, sl + 2:sl + 3], c[:, sl + 1:sl + 2],
                                 AF.Copy, scale=1.0 / DM)
                    s.drain()
                    s.activation(c[:, sl + 3:sl + 4], c[:, sl + 2:sl + 3], AF.Square)
                    s.drain()
                    s.activation(c[:, sl + 4:sl + 5], c[:, sl + 3:sl + 4],
                                 AF.Copy, scale=-1.0, bias=EPS)
                    s.drain()
                    s.activation(c[:, sl + 6:sl + 7], c[:, sl + 0:sl + 1],
                                 AF.Ln, scale=1.0 / DM, bias=c[:, sl + 4:sl + 5])
                    s.drain()
                    jx = s.activation(c[:, sl + 7:sl + 8], c[:, sl + 6:sl + 7],
                                      AF.Exp, scale=-0.5)
                    jx = ev.inc(jx, "ac", 1)
                    ev.mark(f"rstd2_{i2}_{tM}", "ac")

        @blk.vector
        def _(v):
            for i2 in range(nB):
                b, kc = itB(i2)
                par = i2 % 2
                ev.wait(v, f"upe_{i2}")
                v.drain()
                jx = v.tensor_scalar(su_sb[:], ps_ut[0:4, :], scl_sb[0:4, :], None, OP.mult)
                jx = ev.inc(jx, "vc", 1)
                ev.mark(f"su_{i2}", "vc")
                for plo in range(4):
                    ev.wait(v, f"t2pe_{i2}_{plo}")
                    jx = v.tensor_scalar(su2_sb[:, plo * TB:(plo + 1) * TB], ps_ut[0:4, :],
                                         scl_sb[0:4, :], None, OP.mult)
                    jx = ev.inc(jx, "vc", 1)
                    ev.mark(f"su2_{i2}_{plo}", "vc")
                def emit_outn(tM):
                    ev.wait(v, f"rstd2_{i2}_{tM}")
                    if i2 * 4 + tM >= 2:
                        pj = i2 * 4 + tM - 2
                        ev.wait(v, f"outst_{pj // 4}_{pj % 4}")
                    v.drain()
                    jx = v.tensor_scalar(outsb[tM % 2][:], y2pre[tM % 2][:],
                                         ln2col[:, 8 * (tM % 2) + 2:8 * (tM % 2) + 3],
                                         ln2col[:, 8 * (tM % 2) + 7:8 * (tM % 2) + 8],
                                         OP.subtract, OP.mult)
                    jx = ev.inc(jx, "vc", 1)
                    ev.mark(f"outn_{i2}_{tM}", "vc")
                    ev.mark(f"ln2n_{i2}_{tM}", "vc")
                    v.drain()

                for grp in range(8):
                    ps, gq = grp // 4, grp % 4
                    tM = ps * 2 + gq // 2
                    nh = gq % 2
                    ev.wait(v, f"wo2_{i2}_{grp}")
                    if grp >= 1:
                        v.drain()
                    if nh == 0 and (i2 > 0 or tM >= 2):
                        pj = i2 * 4 + tM - 2
                        ev.wait(v, f"ln2n_{pj // 4}_{pj % 4}")
                    jx = v.tensor_tensor(y2pre[tM % 2][:, nh * 512:(nh + 1) * 512],
                                         ps_o[gq][:],
                                         y1cb[par][:, tM * DM + nh * 512:tM * DM + (nh + 1) * 512],
                                         OP.add)
                    jx = ev.inc(jx, "vc", 1)
                    ev.mark(f"o2ev_{i2}_{grp}", "vc")
                    if nh == 1:
                        ev.mark(f"y2add_{i2}_{tM}", "vc")
                        if tM >= 1:
                            emit_outn(tM - 1)
                emit_outn(3)

    stB.close()
    return nc, ev.events


def kernel(L0_attn_x, L0_attn_W_in, L0_attn_conv_w, L0_attn_conv_b, L0_attn_W_x,
           L0_attn_w_dt, L0_attn_b_dt, L0_attn_A_log, L0_attn_D_param, L0_attn_W_out,
           L0_ffn_W1, L0_ffn_b1, L0_ffn_W2, L0_ffn_b2, L0_ffn_scale):
    n_cores = 8
    x = np.ascontiguousarray(np.asarray(L0_attn_x, dtype=np.float32))
    shards = [x[2 * i:2 * i + 2] for i in range(n_cores)]
    common = {
        "W_in": np.ascontiguousarray(L0_attn_W_in, dtype=np.float32),
        "conv_w": np.ascontiguousarray(
            np.asarray(L0_attn_conv_w, dtype=np.float32).reshape(DIN, 4)),
        "conv_b": np.ascontiguousarray(L0_attn_conv_b, dtype=np.float32),
        "W_x": np.ascontiguousarray(L0_attn_W_x, dtype=np.float32),
        "w_dt": np.ascontiguousarray(L0_attn_w_dt, dtype=np.float32),
        "b_dt": np.ascontiguousarray(L0_attn_b_dt, dtype=np.float32),
        "A_log": np.ascontiguousarray(L0_attn_A_log, dtype=np.float32),
        "D_param": np.ascontiguousarray(L0_attn_D_param, dtype=np.float32),
        "W_out": np.ascontiguousarray(L0_attn_W_out, dtype=np.float32),
        "W1": np.ascontiguousarray(L0_ffn_W1, dtype=np.float32),
        "b1": np.ascontiguousarray(L0_ffn_b1, dtype=np.float32),
        "W2": np.ascontiguousarray(L0_ffn_W2, dtype=np.float32),
        "b2": np.ascontiguousarray(L0_ffn_b2, dtype=np.float32),
        "scale": np.ascontiguousarray(
            np.asarray(L0_ffn_scale, dtype=np.float32).reshape(1, 1)),
    }
    _, events = build_nc()
    nc, _ = build_nc(known_events=events)
    in_maps = [dict(common, x=s) for s in shards]
    res = run_bass_kernel_spmd(nc, in_maps, core_ids=list(range(n_cores)))
    return np.concatenate([r["out"] for r in res.results], axis=0)


# version 26 (final)
# speedup vs baseline: 1.1480x; 1.1480x over previous
import contextlib
import numpy as np

import concourse.bass as bass
import concourse.mybir as mybir
from concourse.bass_utils import run_bass_kernel_spmd

F32 = mybir.dt.float32
FP16 = mybir.dt.float16
BF16 = mybir.dt.bfloat16
AF = mybir.ActivationFunctionType
OP = mybir.AluOpType

P = 128
B_LOC = 2
L = 2048
DM = 1024
DIN = 2048
NST = 16
H = 4096
T = 256
NCH = L // T
NDT = DIN // P
NKD = DM // P
NKH = H // P
NMH = H // P
EPS = 1e-5


class Ev:

    def __init__(self, nc, known=None):
        self.nc = nc
        self.sems = {}
        self.cnt = {}
        self.events = {}
        self.known = known

    def sem(self, name):
        if name not in self.sems:
            self.sems[name] = self.nc.alloc_semaphore(name)
            self.cnt[name] = 0
        return self.sems[name]

    def inc(self, instr, name, amt):
        s = self.sem(name)
        self.cnt[name] += amt
        instr.then_inc(s, amt)
        return self.cnt[name]

    def mark(self, ev_name, sem_name):
        self.sem(sem_name)
        self.events[ev_name] = (sem_name, self.cnt[sem_name])

    def wait(self, eng, ev_name):
        tbl = self.known if self.known is not None else self.events
        if ev_name not in tbl:
            if self.known is not None:
                raise KeyError(f"unknown event {ev_name}")
            return
        sname, val = tbl[ev_name]
        if val > 0:
            eng.wait_ge(self.sem(sname), val)


def build_nc(known_events=None, n_iters=B_LOC * NCH, do_a=True, do_b=True):
    nc = bass.Bass("TRN2", target_bir_lowering=False)

    x_in = nc.dram_tensor("x", [B_LOC, L, DM], F32, kind="ExternalInput").ap()
    w_in = nc.dram_tensor("W_in", [2 * DIN, DM], F32, kind="ExternalInput").ap()
    conv_w = nc.dram_tensor("conv_w", [DIN, 4], F32, kind="ExternalInput").ap()
    conv_b = nc.dram_tensor("conv_b", [DIN], F32, kind="ExternalInput").ap()
    w_x = nc.dram_tensor("W_x", [2 * NST + 1, DIN], F32, kind="ExternalInput").ap()
    w_dt = nc.dram_tensor("w_dt", [DIN], F32, kind="ExternalInput").ap()
    b_dt = nc.dram_tensor("b_dt", [DIN], F32, kind="ExternalInput").ap()
    a_log = nc.dram_tensor("A_log", [DIN, NST], F32, kind="ExternalInput").ap()
    d_par = nc.dram_tensor("D_param", [DIN], F32, kind="ExternalInput").ap()
    w_out = nc.dram_tensor("W_out", [DM, DIN], F32, kind="ExternalInput").ap()
    w1 = nc.dram_tensor("W1", [H, DM], F32, kind="ExternalInput").ap()
    b1 = nc.dram_tensor("b1", [H], F32, kind="ExternalInput").ap()
    w2 = nc.dram_tensor("W2", [DM, H], F32, kind="ExternalInput").ap()
    b2 = nc.dram_tensor("b2", [DM], F32, kind="ExternalInput").ap()
    scale_in = nc.dram_tensor("scale", [1, 1], F32, kind="ExternalInput").ap()
    out_hbm = nc.dram_tensor("out", [B_LOC, L, DM], F32, kind="ExternalOutput").ap()

    winT_scr = nc.dram_tensor("winT_scr", [32, NKD, P, P], BF16, kind="Internal").ap()
    win1s_scr = nc.dram_tensor("win1s_scr", [16, P, 4 * NKD * P], BF16, kind="Internal").ap()
    woutT_scr = nc.dram_tensor("woutT_scr", [NDT, P, DM], BF16, kind="Internal").ap()
    w1T_scr = nc.dram_tensor("w1T_scr", [NKD, P, H], BF16, kind="Internal").ap()
    w2T_scr = nc.dram_tensor("w2T_scr", [NKH, P, DM], BF16, kind="Internal").ap()
    y1_scr = nc.dram_tensor("y1_scr", [B_LOC, L, DM], BF16, kind="Internal").ap()
    bc_scr = nc.dram_tensor("bc_scr", [2, 2 * NST + 1, T], F32, kind="Internal").ap()

    ev = Ev(nc, known=known_events)

    ident = nc.alloc_sbuf_tensor("ident", [P, P], BF16)
    wxT_sb = nc.alloc_sbuf_tensor("wxT_sb", [P, NDT * 33], BF16)
    a_sb = nc.alloc_sbuf_tensor("a_sb", [P, NDT * NST], F32)
    wdt_sb = nc.alloc_sbuf_tensor("wdt_sb", [P, NDT], F32)
    bdt_sb = nc.alloc_sbuf_tensor("bdt_sb", [P, NDT], F32)
    dp_sb = nc.alloc_sbuf_tensor("dp_sb", [P, NDT], F32)
    cb_sb = nc.alloc_sbuf_tensor("cb_sb", [P, NDT], F32)
    cw_sb = nc.alloc_sbuf_tensor("cw_sb", [P, NDT * 4], F32)
    b1_sb = nc.alloc_sbuf_tensor("b1_sb", [P, NMH], F32)
    b2_sb = nc.alloc_sbuf_tensor("b2_sb", [1, DM], BF16)
    ones_sb = nc.alloc_sbuf_tensor("ones_sb", [1, P], BF16)
    onesT_sb = nc.alloc_sbuf_tensor("onesT_sb", [1, T], BF16)
    cb_row = nc.alloc_sbuf_tensor("cb_row", [1, DIN], BF16)
    scl_sb = nc.alloc_sbuf_tensor("scl_sb", [P, 1], F32)

    with contextlib.ExitStack() as st0:
        wrow = [st0.enter_context(nc.sbuf_tensor(f"p0_wrow{j}", [P, H], BF16)) for j in range(2)]
        stage = [st0.enter_context(nc.sbuf_tensor(f"p0_stage{j}", [P, NKH * P], BF16)) for j in range(2)]
        wx_nat = st0.enter_context(nc.sbuf_tensor("p0_wx", [2 * NST + 1, DIN], BF16))
        wsc = [st0.enter_context(nc.sbuf_tensor(f"p0_wsc{j}", [P, DM], BF16)) for j in range(4)]
        onesf = st0.enter_context(nc.sbuf_tensor("p0_onesf", [P, P], BF16))
        alog_sb = st0.enter_context(nc.sbuf_tensor("p0_alog", [P, NDT * NST], F32))
        ps_a = st0.enter_context(nc.psum_tensor("p0_psa", [P, 4 * P], BF16))
        ps_b = st0.enter_context(nc.psum_tensor("p0_psb", [P, 4 * P], BF16))

        with nc.Block() as blk:
            @blk.gpsimd
            def _(g):
                g.memset(onesf[:], 1.0)
                g.drain()
                g.affine_select(ident[:], onesf[:], [[-1, P]], OP.is_equal, 0.0,
                                base=0, channel_multiplier=1)
                g.memset(ones_sb[:], 1.0)
                g.memset(onesT_sb[:], 1.0)
                i = g.dma_start(out=cb_row[:], in_=conv_b.unsqueeze(0))
                ev.inc(i, "gw", 16)
                i = g.dma_start(out=wx_nat[:], in_=w_x)
                ev.inc(i, "gw", 16)
                i = g.dma_start(out=b2_sb[:], in_=b2.unsqueeze(0))
                ev.inc(i, "gw", 16)
                i = g.dma_start(out=scl_sb[:], in_=scale_in.broadcast_to([P, 1]))
                ev.inc(i, "gw", 16)
                i = g.drain()
                ev.inc(i, "gc", 1)
                ev.mark("p0_gp", "gc")
                ev.mark("p0_gw", "gw")

            @blk.sync
            def _(sy):
                i = sy.dma_start(out=a_sb[:].rearrange("p (k n) -> p k n", k=NDT),
                                 in_=a_log.rearrange("(k p) n -> p k n", k=NDT))
                ev.inc(i, "hd", 16)
                for src, dst in [(w_dt, wdt_sb), (b_dt, bdt_sb), (d_par, dp_sb),
                                 (conv_b, cb_sb)]:
                    i = sy.dma_start(out=dst[:],
                                     in_=src.rearrange("(k p) -> p k", k=NDT),
                                     allow_slow_non_contiguous=True)
                    ev.inc(i, "hd", 16)
                i = sy.dma_start(out=cw_sb[:].rearrange("p (k c) -> p k c", k=NDT),
                                 in_=conv_w.rearrange("(k p) c -> p k c", k=NDT))
                ev.inc(i, "hd", 16)
                i = sy.dma_start(out=b1_sb[:],
                                 in_=b1.rearrange("(k p) -> p k", k=NMH),
                                 allow_slow_non_contiguous=True)
                ev.inc(i, "hd", 16)
                ev.mark("p0_params", "hd")

        with nc.Block() as blk:
            @blk.scalar
            def _(s):
                ev.wait(s, "p0_params")
                i = s.activation(alog_sb[:], a_sb[:], AF.Exp)
                ev.inc(i, "ac", 1)
                ev.mark("p0_expA", "ac")

            @blk.vector
            def _(v):
                ev.wait(v, "p0_expA")
                i = v.tensor_scalar(a_sb[:], alog_sb[:], -1.0, None, OP.mult)
                ev.inc(i, "vc", 1)
                ev.mark("p0_negA", "vc")

        _prev_nr = {"ws": 16, "win": 32, "wout": 8, "w1": 32, "w2": 8}
        JOBS = [
            ("ws", w_in, 2048, DM, True),
            ("win", w_in, 2 * DIN, DM, False),
            ("wout", w_out, DM, DIN, False),
            ("w1", w1, H, DM, False),
            ("w2", w2, DM, H, False),
        ]

        def scr_write(sy, tag, r, k4, stg):
            if tag == "ws":
                return sy.dma_start(
                    out=win1s_scr[r][:, k4 * NKD * P:(k4 + 1) * NKD * P]
                    .rearrange("p (k c) -> p k c", k=NKD),
                    in_=stg[:, 0:DM].rearrange("p (k c) -> p k c", k=NKD))
            if tag == "win":
                return sy.dma_start(
                    out=winT_scr[r].rearrange("k p c -> p k c"),
                    in_=stg[:, 0:DM].rearrange("p (k c) -> p k c", k=NKD))
            if tag == "wout":
                return sy.dma_start(
                    out=woutT_scr[:, :, r * P:(r + 1) * P].rearrange("k p c -> p k c"),
                    in_=stg[:, 0:DIN].rearrange("p (k c) -> p k c", k=NDT))
            if tag == "w1":
                return sy.dma_start(
                    out=w1T_scr[:, :, r * P:(r + 1) * P].rearrange("k p c -> p k c"),
                    in_=stg[:, 0:DM].rearrange("p (k c) -> p k c", k=NKD))
            return sy.dma_start(
                out=w2T_scr[:, :, r * P:(r + 1) * P].rearrange("k p c -> p k c"),
                in_=stg[:, 0:H].rearrange("p (k c) -> p k c", k=NKH))

        with nc.Block() as blk:
            @blk.tensor
            def _(t):
                ev.wait(t, "p0_gp")
                ev.wait(t, "p0_gw")
                for kt in range(NDT):
                    pst = (ps_a, ps_b)[kt % 2]
                    if kt >= 2:
                        ev.wait(t, f"p0wx_ev{kt - 2}")
                    i = t.transpose(pst[0:P, 0:33],
                                    wx_nat[0:2 * NST + 1, kt * P:(kt + 1) * P],
                                    ident[0:2 * NST + 1, 0:2 * NST + 1])
                    ev.inc(i, "pe", 1)
                    ev.mark(f"p0wx_tr{kt}", "pe")
                g_idx = 0
                for tag, _sa, rows, cols, scaled in JOBS:
                    n_r, n_c = rows // P, cols // P
                    for r in range(n_r):
                        nk4 = 4 if scaled else 1
                        for k4 in range(nk4):
                            if scaled:
                                ev.wait(t, f"{tag}_sc{r}")
                            else:
                                ev.wait(t, f"{tag}_ld{r}")
                            for c in range(n_c):
                                q = g_idx // 4
                                pst = (ps_a, ps_b)[q % 2]
                                if g_idx % 4 == 0:
                                    if q >= 2:
                                        ev.wait(t, f"p0ev_{q - 2}")
                                    else:
                                        ev.wait(t, f"p0wx_ev{NDT - 2}")
                                        ev.wait(t, f"p0wx_ev{NDT - 1}")
                                src_t = wsc[k4] if scaled else wrow[r % 2]
                                i = t.transpose(pst[:, (g_idx % 4) * P:(g_idx % 4 + 1) * P],
                                                src_t[:, c * P:(c + 1) * P], ident[:])
                                ev.inc(i, "pe", 1)
                                if g_idx % 4 == 3:
                                    ev.mark(f"p0tr_{q}", "pe")
                                g_idx += 1
                        ev.mark(f"{tag}_tr{r}", "pe")

            @blk.gpsimd
            def _(g):
                for tag, src_ap, rows, cols, scaled in JOBS:
                    n_r = rows // P
                    for r in range(n_r):
                        if tag == "ws":
                            if r >= 2:
                                ev.wait(g, f"ws_sc{r - 2}")
                        else:
                            if r >= 2:
                                ev.wait(g, f"{tag}_tr{r - 2}")
                            else:
                                ptag = JOBS[[jj[0] for jj in JOBS].index(tag) - 1][0]
                                if ptag == "ws":
                                    ev.wait(g, f"ws_sc{15 - (1 - r)}")
                                else:
                                    ev.wait(g, f"{ptag}_tr{(_prev_nr[ptag]) - (2 - r)}")
                        i = g.dma_start(out=wrow[r % 2][:, 0:cols],
                                        in_=src_ap[r * P:(r + 1) * P, :])
                        ev.inc(i, f"gw{r % 2}", 16)
                        ev.mark(f"{tag}_ld{r}", f"gw{r % 2}")
                g.wait_ge(ev.sem("gw0"), ev.cnt["gw0"])
                g.wait_ge(ev.sem("gw1"), ev.cnt["gw1"])

            @blk.vector
            def _(v):
                ev.wait(v, "p0_params")
                ev.wait(v, "p0_negA")
                for r in range(16):
                    ev.wait(v, f"ws_ld{r}")
                    if r >= 1:
                        ev.wait(v, f"ws_tr{r - 1}")
                    for k4 in range(4):
                        i = v.tensor_scalar(wsc[k4][:], wrow[r % 2][:, 0:DM],
                                            cw_sb[:, r * 4 + k4:r * 4 + k4 + 1],
                                            None, OP.mult)
                    i = v.drain()
                    ev.inc(i, "vc", 1)
                    ev.mark(f"ws_sc{r}", "vc")

            @blk.scalar
            def _(s):
                for kt in range(NDT):
                    pst = (ps_a, ps_b)[kt % 2]
                    ev.wait(s, f"p0wx_tr{kt}")
                    i = s.copy(wxT_sb[:, kt * 33:(kt + 1) * 33], pst[0:P, 0:33])
                    ev.inc(i, "ac", 1)
                    ev.mark(f"p0wx_ev{kt}", "ac")
                g_idx = 0
                st_idx = 0
                for tag, _sa, rows, cols, scaled in JOBS:
                    n_r, n_c = rows // P, cols // P
                    for r in range(n_r):
                        nk4 = 4 if scaled else 1
                        for k4 in range(nk4):
                            if st_idx >= 2:
                                ev.wait(s, f"p0st_{st_idx - 2}")
                            for c in range(0, n_c, 4):
                                q = g_idx // 4
                                pst = (ps_a, ps_b)[q % 2]
                                ev.wait(s, f"p0tr_{q}")
                                i = s.copy(stage[st_idx % 2][:, c * P:(c + 4) * P], pst[:])
                                ev.inc(i, "ac", 1)
                                ev.mark(f"p0ev_{q}", "ac")
                                g_idx += 4
                            ev.mark(f"p0evs_{st_idx}", "ac")
                            st_idx += 1

            @blk.sync
            def _(sy):
                st_idx = 0
                for tag, _sa, rows, cols, scaled in JOBS:
                    n_r = rows // P
                    for r in range(n_r):
                        nk4 = 4 if scaled else 1
                        for k4 in range(nk4):
                            ev.wait(sy, f"p0evs_{st_idx}")
                            i = scr_write(sy, tag, r, k4, stage[st_idx % 2])
                            ev.inc(i, f"hd{st_idx % 2}", 16)
                            ev.mark(f"p0st_{st_idx}", f"hd{st_idx % 2}")
                            st_idx += 1
                        ev.mark(f"{tag}_st{r}a", "hd0")
                        ev.mark(f"{tag}_st{r}b", "hd1")
                    ev.mark(f"{tag}_alldoneA", "hd0")
                    ev.mark(f"{tag}_alldoneB", "hd1")
                sy.wait_ge(ev.sem("hd0"), ev.cnt["hd0"])
                sy.wait_ge(ev.sem("hd1"), ev.cnt["hd1"])

    nc.all_engine_barrier()

    stA = contextlib.ExitStack()
    tsb = lambda nm, sh, dt: stA.enter_context(nc.sbuf_tensor(nm, sh, dt))
    TE = 4 + T
    w1s_ring = [tsb(f"w1s_r{j}", [P, 4 * NKD * P], BF16) for j in range(2)]
    win_ring = [tsb(f"win_r{j}", [P, NKD * P], BF16) for j in range(2)]
    wout_ring = [tsb(f"wout_r{j}", [P, DM], BF16) for j in range(2)]
    x_bf = [tsb(f"x_bf{j}", [P, 2 * DM], BF16) for j in range(3)]
    xT = [tsb(f"xT{j}", [P, NKD * TE], BF16) for j in range(2)]
    xc = [tsb(f"xc{j}", [P, NDT * T], BF16) for j in range(2)]
    zsil = [tsb(f"zsil{j}", [P, NDT * T], BF16) for j in range(2)]
    vv = [tsb(f"vv{j}", [P, NDT * T], BF16) for j in range(2)]
    brep = [tsb(f"brep{j}", [P, NST * T], BF16) for j in range(2)]
    crep = [tsb(f"crep{j}", [P, NST * T], BF16) for j in range(2)]
    dtrep = [tsb(f"dtrep{j}", [P, T], F32) for j in range(2)]
    delta = [tsb(f"delta{j}", [P, T], F32) for j in range(8)]
    dA = [tsb(f"dA{j}", [P, NST * T], FP16) for j in range(2)]
    dbx = [tsb(f"dbx{j}", [P, NST * T], BF16) for j in range(1)]
    hc = tsb("hc", [P, NST * T], BF16)
    fixb = tsb("fixb", [P, NST], F32)
    scar = tsb("scar", [P, NDT * NST], F32)
    gtmp = tsb("gtmp", [P, T], F32)
    ybuf = [tsb(f"ybuf{j}", [P, NDT * T], BF16) for j in range(2)]
    y1pre = [tsb(f"y1pre{j}", [P, DM], F32) for j in range(2)]
    bcsb = tsb("bcsb", [2 * NST + 1, T], F32)
    lncol = tsb("lncol", [P, 16], F32)
    ps_tr = stA.enter_context(nc.psum_tensor("ps_tr", [P, P], BF16))
    ps_xz = [stA.enter_context(nc.psum_tensor(f"ps_xz{j}", [P, T], F32)) for j in range(2)]
    ps_xp = stA.enter_context(nc.psum_tensor("ps_xp", [2 * NST + 1, T], F32))
    ps_wo = [stA.enter_context(nc.psum_tensor(f"ps_wo{j}", [P, 512], F32)) for j in range(4)]

    def it_bk(i):
        return i // NCH, i % NCH

    def emit_wout_kt(t, i, kt):
        par = i % 2
        ev.wait(t, f"ydt_{i}_{kt}")
        if kt == 0 and i > 0:
            for grp in range(4):
                ev.wait(t, f"woev_{i - 1}_{grp}")
        jj = i * NDT + kt
        ev.wait(t, f"wold_{jj}")
        for grp in range(4):
            tM, nh = grp // 2, grp % 2
            jx = t.matmul(ps_wo[grp][:],
                          ybuf[par][:, kt * T + tM * P:kt * T + (tM + 1) * P],
                          wout_ring[jj % 2][:, nh * 512:(nh + 1) * 512],
                          start=(kt == 0), stop=(kt == NDT - 1))
        ev.inc(jx, "pe", 1)
        ev.mark(f"woK_{jj}", "pe")
        if kt == NDT - 1:
            ev.mark(f"wo_{i}", "pe")

    def emit_ln1_evac(v, i):
        par = i % 3
        v.drain()
        for grp in range(4):
            tM, nh = grp // 2, grp % 2
            ev.wait(v, f"wo_{i}")
            jx = v.tensor_tensor(y1pre[tM][:, nh * 512:(nh + 1) * 512],
                                 ps_wo[grp][:],
                                 x_bf[par][:, tM * DM + nh * 512:tM * DM + (nh + 1) * 512],
                                 OP.add)
            jx = ev.inc(jx, "vc", 1)
            ev.mark(f"woev_{i}_{grp}", "vc")
        ev.mark(f"ln_add_{i}", "vc")
        for tM in range(2):
            ev.mark(f"lnadd_{i}_{tM}", "vc")
        v.drain()

    def emit_ln1_norm(v, i):
        par = i % 3
        v.drain()
        if i >= 2:
            ev.wait(v, f"y1st_{i - 2}")
        for tM in range(2):
            ev.wait(v, f"rstd_{i}_{tM}")
            jx = v.tensor_scalar(x_bf[par][:, tM * DM:(tM + 1) * DM], y1pre[tM][:],
                                 lncol[:, 8 * tM + 2:8 * tM + 3],
                                 lncol[:, 8 * tM + 7:8 * tM + 8],
                                 OP.subtract, OP.mult)
        jx = ev.inc(jx, "vc", 1)
        ev.mark(f"ln_norm_{i}", "vc")
        v.drain()

    def emit_lnstat(s, i):
        c = lncol
        for tM in range(2):
            ev.wait(s, f"lnadd_{i}_{tM}")
            dmp = ybuf[i % 2][:, 0:2 * DM].bitcast(F32)
            jx = s.activation(dmp, y1pre[tM][:], AF.Square,
                              accum_out=c[:, 8 * tM + 0:8 * tM + 1])
            s.drain()
            jx = s.activation(dmp, y1pre[tM][:], AF.Copy,
                              accum_out=c[:, 8 * tM + 1:8 * tM + 2])
            s.drain()
        for tM in range(2):
            s.activation(c[:, 8 * tM + 2:8 * tM + 3], c[:, 8 * tM + 1:8 * tM + 2],
                         AF.Copy, scale=1.0 / DM)
        s.drain()
        for tM in range(2):
            s.activation(c[:, 8 * tM + 3:8 * tM + 4], c[:, 8 * tM + 2:8 * tM + 3],
                         AF.Square)
        s.drain()
        for tM in range(2):
            s.activation(c[:, 8 * tM + 4:8 * tM + 5], c[:, 8 * tM + 3:8 * tM + 4],
                         AF.Copy, scale=-1.0, bias=EPS)
        s.drain()
        for tM in range(2):
            s.activation(c[:, 8 * tM + 6:8 * tM + 7], c[:, 8 * tM + 0:8 * tM + 1],
                         AF.Ln, scale=1.0 / DM, bias=c[:, 8 * tM + 4:8 * tM + 5])
        s.drain()
        for tM in range(2):
            jx = s.activation(c[:, 8 * tM + 7:8 * tM + 8], c[:, 8 * tM + 6:8 * tM + 7],
                              AF.Exp, scale=-0.5)
        ev.inc(jx, "ac", 1)
        for tM in range(2):
            ev.mark(f"rstd_{i}_{tM}", "ac")

    if not do_a:
        n_iters = 0
    with nc.Block() as blk:
        @blk.sync
        def _(sy):
            def wout_slabs(i):
                for kt in range(NDT):
                    jj = i * NDT + kt
                    if jj >= 2:
                        ev.wait(sy, f"woK_{jj - 2}")
                    j = sy.dma_start(out=wout_ring[jj % 2][:], in_=woutT_scr[kt])
                    ev.inc(j, f"wor{jj % 2}", 16)
                    ev.mark(f"wold_{jj}", f"wor{jj % 2}")

            for e in ["ws_alldoneA", "ws_alldoneB", "win_alldoneA", "win_alldoneB",
                      "wout_alldoneA", "wout_alldoneB"]:
                ev.wait(sy, e)
            for i in range(n_iters):
                for m in range(NDT):
                    jj = i * NDT + m
                    if jj >= 2:
                        ev.wait(sy, f"xzMx_{jj - 2}")
                    j = sy.dma_start(out=w1s_ring[jj % 2][:], in_=win1s_scr[m])
                    ev.inc(j, f"w1r{jj % 2}", 16)
                    ev.mark(f"xsl_{jj}", f"w1r{jj % 2}")
                for m in range(NDT):
                    jj = i * NDT + m
                    if jj >= 2:
                        ev.wait(sy, f"xzMz_{jj - 2}")
                    j = sy.dma_start(out=win_ring[jj % 2][:].rearrange("p (k c) -> p k c", k=NKD),
                                     in_=winT_scr[NDT + m].rearrange("k p c -> p k c"))
                    ev.inc(j, f"wzr{jj % 2}", 16)
                    ev.mark(f"xszl_{jj}", f"wzr{jj % 2}")
                if i > 0:
                    wout_slabs(i - 1)
            wout_slabs(n_iters - 1)

        def emit_y1st(g, i):
            b, k = it_bk(i)
            ev.wait(g, f"ln_norm_{i}")
            j = g.dma_start(out=y1_scr[b, k * T:(k + 1) * T, :]
                            .rearrange("(tt p) d -> p tt d", tt=2),
                            in_=x_bf[i % 3][:].rearrange("p (tt d) -> p tt d", tt=2))
            ev.inc(j, f"gy{i % 2}", 16)
            ev.mark(f"y1st_{i}", f"gy{i % 2}")

        def emit_xcast(g, i):
            b, k = it_bk(i)
            if i >= 3:
                ev.wait(g, f"y1st_{i - 3}")
            j = g.dma_start(out=x_bf[i % 3][:].rearrange("p (tt d) -> p tt d", tt=2),
                            in_=x_in[b, k * T:(k + 1) * T, :]
                            .rearrange("(tt p) d -> p tt d", tt=2))
            ev.inc(j, f"gx{i % 2}", 16)
            ev.mark(f"xld_{i}", f"gx{i % 2}")

        @blk.gpsimd
        def _(g):
            ev.wait(g, "p0_params")
            emit_xcast(g, 0)
            for i in range(n_iters):
                b, k = it_bk(i)
                par = i % 2
                ev.wait(g, f"xpev_{i}")
                j = g.dma_start(out=bc_scr[par], in_=bcsb[:])
                ev.inc(j, "gb", 16)
                ev.mark(f"bounce_{i}", "gb")
                g.wait_ge(ev.sem("gb"), ev.cnt["gb"])
                if i >= 2:
                    ev.wait(g, f"dloop_{i - 2}")
                j = g.dma_start(out=brep[par][:].rearrange("p (n t) -> p n t", n=NST),
                                in_=bc_scr[par][0:NST].unsqueeze(0).broadcast_to([P, NST, T]))
                ev.inc(j, "gr", 16)
                j = g.dma_start(out=crep[par][:].rearrange("p (n t) -> p n t", n=NST),
                                in_=bc_scr[par][NST:2 * NST].unsqueeze(0).broadcast_to([P, NST, T]))
                ev.inc(j, "gr", 16)
                j = g.dma_start(out=dtrep[par][:],
                                in_=bc_scr[par][2 * NST:2 * NST + 1].broadcast_to([P, T]))
                ev.inc(j, "gr", 16)
                ev.mark(f"rep_{i}", "gr")
                def vhalf(h):
                    ev.wait(g, f"deltaH_{i}_{h}")
                    ev.wait(g, f"siluxc_{i}_{h * 8 + 7}")
                    if i >= 2 and h == 0:
                        ev.wait(g, f"dloop_{i - 2}")
                    for dt in range(h * 8, h * 8 + 8):
                        j = g.tensor_tensor(vv[par][:, dt * T:(dt + 1) * T],
                                            delta[dt % 8][:],
                                            xc[par][:, dt * T:(dt + 1) * T], OP.mult)
                    j = g.drain()
                    ev.inc(j, "gc", 1)
                    ev.mark(f"vH_{i}_{h}", "gc")
                if i + 1 < n_iters:
                    emit_xcast(g, i + 1)
                vhalf(0)
                if i > 0:
                    emit_y1st(g, i - 1)
                vhalf(1)
            if n_iters > 0:
                emit_y1st(g, n_iters - 1)
            g.wait_ge(ev.sem("gy0"), ev.cnt["gy0"])
            g.wait_ge(ev.sem("gy1"), ev.cnt["gy1"])


        @blk.tensor
        def _(t):
            ev.wait(t, "p0_gp")
            ev.wait(t, "p0_params")
            ev.wait(t, "p0_gw")
            for i in range(n_iters):
                par = i % 2
                ev.wait(t, f"xld_{i}")
                for j2 in range(16):
                    tt, dd = j2 // NKD, j2 % NKD
                    jj = i * 16 + j2
                    if jj >= 1:
                        ev.wait(t, f"trev_{(jj - 1) // 16}_{(jj - 1) % 16}")
                    jx = t.transpose(ps_tr[:],
                                     x_bf[i % 3][:, tt * DM + dd * P:tt * DM + (dd + 1) * P],
                                     ident[:])
                    ev.inc(jx, "pe", 1)
                    ev.mark(f"tr_{i}_{j2}", "pe")
                ev.wait(t, f"trev_{i}_15")
                for m in range(NDT):
                    jj = i * NDT + m
                    ev.wait(t, f"xsl_{jj}")
                    pst = ps_xz[m % 2]
                    if i * 32 + m >= 2:
                        j3 = i * 32 + m - 2
                        ev.wait(t, f"xzev_{j3 // 32}_{j3 % 32}")
                    for kt in range(NKD):
                        for k4 in range(4):
                            jx = t.matmul(pst[:],
                                          w1s_ring[jj % 2][:, (k4 * NKD + kt) * P:(k4 * NKD + kt + 1) * P],
                                          xT[par][:, kt * TE + 1 + k4:kt * TE + 1 + k4 + T],
                                          start=(kt == 0 and k4 == 0), stop=False)
                    jx = t.matmul(pst[:], cb_row[0:1, m * P:(m + 1) * P],
                                  onesT_sb[0:1, 0:T], start=False, stop=True)
                    ev.inc(jx, "pe", 1)
                    ev.mark(f"xzMx_{jj}", "pe")
                    ev.mark(f"xzMl_{i}_{m}", "pe")
                for m in range(NDT):
                    jj = i * NDT + m
                    ev.wait(t, f"xszl_{jj}")
                    pst = ps_xz[m % 2]
                    j3 = i * 32 + NDT + m - 2
                    ev.wait(t, f"xzev_{j3 // 32}_{j3 % 32}")
                    for kt in range(NKD):
                        jx = t.matmul(pst[:], win_ring[jj % 2][:, kt * P:(kt + 1) * P],
                                      xT[par][:, kt * TE + 4:kt * TE + 4 + T],
                                      start=(kt == 0), stop=(kt == NKD - 1))
                    ev.inc(jx, "pe", 1)
                    ev.mark(f"xzMz_{jj}", "pe")
                    ev.mark(f"xzMl_{i}_{NDT + m}", "pe")
                ev.wait(t, f"siluxc_{i}_{NDT - 1}")
                if i > 0:
                    ev.wait(t, f"xpev_{i - 1}")
                for kt in range(NDT):
                    jx = t.matmul(ps_xp[:], wxT_sb[:, kt * 33:(kt + 1) * 33],
                                  xc[par][:, kt * T:(kt + 1) * T],
                                  start=(kt == 0), stop=(kt == NDT - 1))
                ev.inc(jx, "pe", 1)
                ev.mark(f"xp_{i}", "pe")
                if i > 0:
                    for kt in range(NDT):
                        emit_wout_kt(t, i - 1, kt)
            if n_iters > 0:
                for kt in range(NDT):
                    emit_wout_kt(t, n_iters - 1, kt)

        @blk.scalar
        def _(s):
            ev.wait(s, "p0_params")
            ev.wait(s, "p0_negA")
            ev.wait(s, "p0_gw")

            def prep_tails(i):
                b, k = it_bk(i)
                par = i % 2
                s.drain()
                if i >= 2:
                    ev.wait(s, f"xzMz_{(i - 1) * NDT + NDT - 1}")
                for dd in range(NKD):
                    if k == 0:
                        s.memzero(xT[par][:, dd * TE:dd * TE + 4])
                    else:
                        s.copy(xT[par][:, dd * TE + 1:dd * TE + 4],
                               xT[1 - par][:, dd * TE + 4 + T - 3:dd * TE + 4 + T])
                s.drain()

            def prep_trev(i, j2):
                tt, dd = j2 // NKD, j2 % NKD
                par = i % 2
                ev.wait(s, f"tr_{i}_{j2}")
                jx = s.copy(xT[par][:, dd * TE + 4 + tt * P:dd * TE + 4 + (tt + 1) * P],
                            ps_tr[:])
                jx = ev.inc(jx, "ac", 1)
                ev.mark(f"trev_{i}_{j2}", "ac")

            def prep_evac(i, m):
                b, k = it_bk(i)
                par = i % 2
                ev.wait(s, f"xzMl_{i}_{m}")
                pst = ps_xz[m % 2]
                if m < NDT:
                    dt = m
                    if i >= 2 and dt == 0:
                        ev.wait(s, f"xp_{i - 2}")
                        ev.wait(s, f"dloop_{i - 2}")
                    jx = s.activation(xc[par][:, dt * T:(dt + 1) * T], pst[:], AF.Silu)
                    ev.inc(jx, "ac", 1)
                    ev.mark(f"siluxc_{i}_{dt}", "ac")
                    ev.mark(f"xzev_{i}_{m}", "ac")
                else:
                    dt = m - NDT
                    if i >= 2 and dt == 0:
                        ev.wait(s, f"dloop_{i - 2}")
                    jx = s.activation(zsil[par][:, dt * T:(dt + 1) * T], pst[:], AF.Silu)
                    ev.inc(jx, "ac", 1)
                    ev.mark(f"zsil_{i}_{dt}", "ac")
                    ev.mark(f"xzev_{i}_{m}", "ac")

            def prep_xpev(i):
                ev.wait(s, f"xp_{i}")
                if i > 0:
                    ev.wait(s, f"bounce_{i - 1}")
                jx = s.copy(bcsb[:], ps_xp[:])
                jx = ev.inc(jx, "ac", 1)
                ev.mark(f"xpev_{i}", "ac")

            def prep_delta_q(i, q):
                par = i % 2
                ev.wait(s, f"rep_{i}")
                if q == 0 and i > 0:
                    ev.wait(s, f"vH_{i - 1}_{1}")
                if q == 2:
                    ev.wait(s, f"vH_{i}_{0}")
                s.drain()
                for dt in range(q * 4, q * 4 + 4):
                    s.activation(delta[dt % 8][:], dtrep[par][:], AF.Exp,
                                 bias=bdt_sb[:, dt:dt + 1], scale=wdt_sb[:, dt:dt + 1])
                s.drain()
                for dt in range(q * 4, q * 4 + 4):
                    jx = s.activation(delta[dt % 8][:], delta[dt % 8][:], AF.Ln, bias=1.0)
                s.drain()
                if q % 2 == 1:
                    jx = ev.inc(jx, "ac", 1)
                    ev.mark(f"deltaH_{i}_{q // 2}", "ac")

            def emit_dA(i, dt):
                jd = i * NDT + dt
                da_t = dA[jd % 2]
                if jd >= 2:
                    ev.wait(s, f"scan_{jd - 2}")
                for n in range(NST):
                    jx = s.activation(da_t[:, n * T:(n + 1) * T], delta[dt % 8][:],
                                      AF.Exp,
                                      scale=a_sb[:, dt * NST + n:dt * NST + n + 1])
                jx = ev.inc(jx, "ac", 1)
                ev.mark(f"dA_{jd}", "ac")

            if n_iters > 0:
                prep_tails(0)
                for j2 in range(16):
                    prep_trev(0, j2)
                for m in range(32):
                    prep_evac(0, m)
                prep_xpev(0)
                prep_delta_q(0, 0)
                prep_delta_q(0, 1)

            for i in range(n_iters):
                nxt = i + 1 if i + 1 < n_iters else None
                for dt in range(NDT):
                    emit_dA(i, dt)
                    if dt == 0 and i > 0:
                        emit_lnstat(s, i - 1)
                    if dt == 4:
                        prep_delta_q(i, 2)
                    if dt == 7:
                        prep_delta_q(i, 3)
                    if nxt is not None:
                        if dt == 0:
                            prep_tails(nxt)
                        if 2 <= dt < 6:
                            for j2 in range(4 * (dt - 2), 4 * (dt - 2) + 4):
                                prep_trev(nxt, j2)
                        elif 6 <= dt <= 12:
                            sched = [5, 5, 5, 5, 4, 4, 4]
                            m0 = sum(sched[:dt - 6])
                            for mm in range(m0, m0 + sched[dt - 6]):
                                prep_evac(nxt, mm)
                        if dt == 13:
                            prep_xpev(nxt)
                if nxt is not None:
                    prep_delta_q(nxt, 0)
                    prep_delta_q(nxt, 1)
            if n_iters > 0:
                emit_lnstat(s, n_iters - 1)

        @blk.vector
        def _(v):
            ev.wait(v, "p0_params")
            for i in range(n_iters):
                b, k = it_bk(i)
                par = i % 2
                if i > 0:
                    emit_ln1_evac(v, i - 1)
                    emit_ln1_norm(v, i - 1)
                for dt in range(NDT):
                    if dt % 8 == 0:
                        ev.wait(v, f"vH_{i}_{dt // 8}")
                        ev.wait(v, f"zsil_{i}_{dt // 8 * 8 + 7}")
                    jd = i * NDT + dt
                    ev.wait(v, f"dA_{jd}")
                    jx = v.tensor_tensor(
                        dbx[0][:].rearrange("p (n t) -> p n t", n=NST),
                        vv[par][:, dt * T:(dt + 1) * T].unsqueeze(1)
                        .broadcast_to([P, NST, T]),
                        brep[par][:].rearrange("p (n t) -> p n t", n=NST),
                        OP.mult)
                    v.drain()
                    if k > 0:
                        jx = v.tensor_tensor(
                            fixb[:],
                            dA[jd % 2][:].rearrange("p (n t) -> p n t", n=NST)[:, :, 0],
                            scar[:, dt * NST:(dt + 1) * NST],
                            OP.mult)
                        v.drain()
                        jx = v.tensor_tensor(
                            dbx[0][:].rearrange("p (n t) -> p n t", n=NST)[:, :, 0],
                            dbx[0][:].rearrange("p (n t) -> p n t", n=NST)[:, :, 0],
                            fixb[:], OP.add)
                    jx = v.memset(dA[jd % 2][:]
                                  .rearrange("p (n t) -> p n t", n=NST)[:, :, 0:1], 0.0)
                    v.drain()
                    HT = NST * T // 2
                    jx = v.tensor_tensor_scan(hc[:, 0:HT], dA[jd % 2][:, 0:HT],
                                              dbx[0][:, 0:HT], 0.0, OP.mult, OP.add)
                    jx = v.tensor_tensor_scan(hc[:, HT:], dA[jd % 2][:, HT:],
                                              dbx[0][:, HT:], 0.0, OP.mult, OP.add)
                    jx = ev.inc(jx, "vc", 1)
                    ev.mark(f"scan_{jd}", "vc")
                    v.drain()
                    jx = v.tensor_copy(
                        scar[:, dt * NST:(dt + 1) * NST],
                        hc[:].rearrange("p (n t) -> p n t", n=NST)[:, :, T - 1])
                    v.drain()
                    jx = v.tensor_tensor(hc[:], hc[:], crep[par][:], OP.mult)
                    v.drain()
                    jx = v.tensor_add(hc[:, 0:8 * T], hc[:, 0:8 * T], hc[:, 8 * T:16 * T])
                    v.drain()
                    jx = v.tensor_add(hc[:, 0:4 * T], hc[:, 0:4 * T], hc[:, 4 * T:8 * T])
                    v.drain()
                    jx = v.tensor_add(hc[:, 0:2 * T], hc[:, 0:2 * T], hc[:, 2 * T:4 * T])
                    v.drain()
                    jx = v.tensor_add(hc[:, 0:T], hc[:, 0:T], hc[:, T:2 * T])
                    v.drain()
                    jx = v.scalar_tensor_tensor(gtmp[:], xc[par][:, dt * T:(dt + 1) * T],
                                                dp_sb[:, dt:dt + 1], hc[:, 0:T],
                                                OP.mult, OP.add)
                    v.drain()
                    jx = v.tensor_tensor(ybuf[par][:, dt * T:(dt + 1) * T], gtmp[:],
                                         zsil[par][:, dt * T:(dt + 1) * T], OP.mult)
                    jx = ev.inc(jx, "vc", 1)
                    ev.mark(f"ydt_{i}_{dt}", "vc")
                    v.drain()
                ev.mark(f"dloop_{i}", "vc")
            if n_iters > 0:
                emit_ln1_evac(v, n_iters - 1)
                emit_ln1_norm(v, n_iters - 1)

    stA.close()
    nc.all_engine_barrier()
    if not do_b:
        return nc, ev.events

    TB = 512
    NCB = L // TB
    nB = B_LOC * NCB
    stB = contextlib.ExitStack()
    tsb2 = lambda nm, sh, dt: stB.enter_context(nc.sbuf_tensor(nm, sh, dt))
    w1t_sb = tsb2("w1t_sb", [P, NKD * H], BF16)
    w2_ring = [tsb2(f"w2_r{j}", [P, DM], BF16) for j in range(4)]
    y1cb = [tsb2(f"y1cb{j}", [P, 4 * DM], BF16) for j in range(2)]
    y1T = [tsb2(f"y1T{j}", [P, NKD * TB], BF16) for j in range(2)]
    h_sb = tsb2("h_sb", [P, NKH * TB], BF16)
    ltT = tsb2("ltT", [P, NKD * NST], BF16)
    ltok = tsb2("ltok", [NST, DM], BF16)
    e1z = tsb2("e1z", [4, H], BF16)
    e2z = tsb2("e2z", [4, 4 * DM], BF16)
    su_sb = tsb2("su_sb", [4, TB], BF16)
    su2_sb = tsb2("su2_sb", [4, 4 * TB], BF16)
    y2pre = [tsb2(f"y2pre{j}", [P, DM], F32) for j in range(2)]
    outsb = [tsb2(f"outsb{j}", [P, DM], F32) for j in range(2)]
    ln2col = tsb2("ln2col", [P, 16], F32)
    lnjB = tsb2("lnjB", [P, DM], F32)
    ps_trB = stB.enter_context(nc.psum_tensor("ps_trB", [P, P], BF16))
    ps_h = [stB.enter_context(nc.psum_tensor(f"ps_h{j}", [P, TB], F32)) for j in range(2)]
    ps_ut = stB.enter_context(nc.psum_tensor("ps_ut", [4, TB], F32))
    ps_o = [stB.enter_context(nc.psum_tensor(f"ps_oB{j}", [P, 512], F32)) for j in range(4)]

    def itB(i):
        return i // NCB, i % NCB

    with nc.Block() as blk:
        @blk.sync
        def _(sy):
            for e in ["w1_alldoneA", "w1_alldoneB", "w2_alldoneA", "w2_alldoneB"]:
                ev.wait(sy, e)
            def w1chunk(kt):
                i = sy.dma_start(out=w1t_sb[:, kt * H:(kt + 1) * H], in_=w1T_scr[kt])
                ev.inc(i, "hd", 16)

            def w2slab(jj):
                kt = jj % NKH
                if jj >= 4:
                    ev.wait(sy, f"wo2K_{jj - 4}")
                j = sy.dma_start(out=w2_ring[jj % 4][:], in_=w2T_scr[kt])
                ev.inc(j, f"w2r{jj % 4}", 16)
                ev.mark(f"w2ld_{jj}", f"w2r{jj % 4}")

            w1chunk(0)
            w1chunk(1)
            w2slab(0)
            w2slab(1)
            for kt in range(2, NKD):
                w1chunk(kt)
            ev.mark("bw_ld", "hd")
            w2slab(2)
            w2slab(3)
            def emit_outst(i2):
                b, kc = itB(i2)
                for tM in range(4):
                    ev.wait(sy, f"outn_{i2}_{tM}")
                    j = sy.dma_start(
                        out=out_hbm[b, kc * TB + tM * P:kc * TB + (tM + 1) * P, :],
                        in_=outsb[tM % 2][:])
                    ev.inc(j, f"out{tM % 2}", 16)
                    ev.mark(f"outst_{i2}_{tM}", f"out{tM % 2}")

            for i2 in range(nB):
                if i2 > 0:
                    emit_outst(i2 - 1)
                for ps in range(2):
                    for kt in range(NKH):
                        jj = (i2 * 2 + ps) * NKH + kt
                        if jj >= 4:
                            w2slab(jj)
            emit_outst(nB - 1)
            sy.wait_ge(ev.sem("hd"), ev.cnt["hd"])
            sy.wait_ge(ev.sem("out0"), ev.cnt["out0"])
            sy.wait_ge(ev.sem("out1"), ev.cnt["out1"])

        @blk.gpsimd
        def _(g):
            for i2 in range(nB):
                b, kc = itB(i2)
                par = i2 % 2
                if kc == 0:
                    j = g.dma_start(out=ltok[:], in_=y1_scr[b, 1:NST + 1, :])
                    ev.inc(j, "gyt", 16)
                    ev.mark(f"ltok_{b}", "gyt")
                if i2 >= 2:
                    ev.wait(g, f"y2add_{i2 - 2}_3")
                j = g.dma_start(out=y1cb[par][:].rearrange("p (tt d) -> p tt d", tt=4),
                                in_=y1_scr[b, kc * TB:(kc + 1) * TB, :]
                                .rearrange("(tt p) d -> p tt d", tt=4))
                ev.inc(j, f"gyl{par}", 16)
                ev.mark(f"y1ld_{i2}", f"gyl{par}")
                if kc == 0:
                    if i2 > 0:
                        ev.wait(g, f"wo2_{i2 - 1}_7")
                    g.memset(e1z[:], 0.0)
                    g.memset(e2z[:], 0.0)
                    j = g.drain()
                    ev.inc(j, "gc", 1)
                    ev.mark(f"e2pm_{b}", "gc")
                    for i4 in range(4):
                        j = g.dma_start(out=e1z[i4:i4 + 1, i4 * DM:(i4 + 1) * DM],
                                        in_=y1_scr[b, 1 + 2 * i4:2 + 2 * i4, :])
                        ev.inc(j, "ge2", 16)
                        for plo in range(4):
                            j = g.dma_start(
                                out=e2z[i4:i4 + 1, plo * DM + i4 * T:plo * DM + (i4 + 1) * T],
                                in_=y1_scr[b, 9 + 2 * i4:10 + 2 * i4, plo:DM:4],
                                allow_slow_non_contiguous=True)
                            ev.inc(j, "ge2", 16)
                    ev.mark(f"e2p_{b}", "ge2")

        @blk.tensor
        def _(t):
            for i2 in range(nB):
                b, kc = itB(i2)
                par = i2 % 2
                if kc == 0:
                    ev.wait(t, f"ltok_{b}")
                    for dd in range(NKD):
                        if b > 0 or dd >= 1:
                            pj = (b * NKD + dd - 1)
                            ev.wait(t, f"ltT_ev_{pj // NKD}_{pj % NKD}" if dd >= 1
                                    else f"y1T_ev_{i2 - 1}_31")
                        jx = t.transpose(ps_trB[0:P, 0:NST],
                                         ltok[0:NST, dd * P:(dd + 1) * P],
                                         ident[0:NST, 0:NST])
                        ev.inc(jx, "pe", 1)
                        ev.mark(f"ltT_tr_{b}_{dd}", "pe")
                ev.wait(t, f"y1ld_{i2}")
                for j2 in range(32):
                    tt, dd = j2 // NKD, j2 % NKD
                    jj = i2 * 32 + j2
                    if kc == 0 and j2 == 0:
                        ev.wait(t, f"ltT_ev_{b}_{NKD - 1}")
                    elif jj >= 1:
                        ev.wait(t, f"y1T_ev_{(jj - 1) // 32}_{(jj - 1) % 32}")
                    jx = t.transpose(ps_trB[:],
                                     y1cb[par][:, tt * DM + dd * P:tt * DM + (dd + 1) * P],
                                     ident[:])
                    ev.inc(jx, "pe", 1)
                    ev.mark(f"y1T_tr_{i2}_{j2}", "pe")
                ev.wait(t, f"y1T_ev_{i2}_31")
                ev.wait(t, f"ltTev_{b}")
                if i2 > 0:
                    ev.wait(t, f"su_{i2 - 1}")
                for kt in range(NKD):
                    jx = t.matmul(ps_ut[:], ltT[:, kt * NST + 1:kt * NST + 8:2],
                                  y1T[par][:, kt * TB:(kt + 1) * TB],
                                  start=(kt == 0), stop=(kt == NKD - 1))
                jx = ev.inc(jx, "pe", 1)
                ev.mark(f"upe_{i2}", "pe")
                ev.wait(t, f"su_{i2}")
                ev.wait(t, f"e2p_{b}")
                ev.wait(t, f"e2pm_{b}")
                if i2 == 0:
                    ev.wait(t, "bw_ld")
                for m in range(NMH):
                    pst = ps_h[m % 2]
                    jj = i2 * NMH + m - 2
                    if jj >= 0:
                        ev.wait(t, f"hev_{jj // NMH}_{jj % NMH}")
                    for kt in range(NKD):
                        jx = t.matmul(pst[:], w1t_sb[:, kt * H + m * P:kt * H + (m + 1) * P],
                                      y1T[par][:, kt * TB:(kt + 1) * TB],
                                      start=(kt == 0), stop=False)
                    jx = t.matmul(pst[:], e1z[0:4, m * P:(m + 1) * P],
                                  su_sb[0:4, :], start=False, stop=True)
                    ev.inc(jx, "pe", 1)
                    ev.mark(f"hpe_{i2}_{m}", "pe")
                for plo in range(4):
                    ev.wait(t, f"hev_{i2}_{plo * 8 + 7}")
                    jj = i2 * 4 + plo - 1
                    if jj >= 0:
                        ev.wait(t, f"su2_{jj // 4}_{jj % 4}")
                    for qt in range(NKD):
                        jx = t.matmul(ps_ut[:], ltT[:, qt * NST + 9:qt * NST + 16:2],
                                      h_sb[:, (plo * NKD + qt) * TB:(plo * NKD + qt + 1) * TB],
                                      start=(qt == 0), stop=(qt == NKD - 1))
                    jx = ev.inc(jx, "pe", 1)
                    ev.mark(f"t2pe_{i2}_{plo}", "pe")
                ev.wait(t, f"hev_{i2}_{NMH - 1}")
                for plo in range(4):
                    ev.wait(t, f"su2_{i2}_{plo}")
                for ps in range(2):
                    for kt in range(NKH):
                        jj = (i2 * 2 + ps) * NKH + kt
                        ev.wait(t, f"w2ld_{jj}")
                        for gq in range(4):
                            tM = ps * 2 + gq // 2
                            nh = gq % 2
                            if kt == 0:
                                pj = i2 * 8 + ps * 4 + gq - 4
                                if pj >= 0:
                                    ev.wait(t, f"o2ev_{pj // 8}_{pj % 8}")
                            jx = t.matmul(
                                ps_o[gq][:],
                                h_sb[:, kt * TB + tM * P:kt * TB + (tM + 1) * P],
                                w2_ring[jj % 4][:, nh * 512:(nh + 1) * 512],
                                start=(kt == 0), stop=False)
                        ev.inc(jx, "pe", 1)
                        ev.mark(f"wo2K_{jj}", "pe")
                    for gq in range(4):
                        tM = ps * 2 + gq // 2
                        nh = gq % 2
                        grp = ps * 4 + gq
                        for plo in range(4):
                            jx = t.matmul(ps_o[gq][:],
                                          su2_sb[0:4, plo * TB + tM * P:plo * TB + (tM + 1) * P],
                                          e2z[0:4, plo * DM + nh * 512:plo * DM + (nh + 1) * 512],
                                          start=False, stop=False)
                        jx = t.matmul(ps_o[gq][:], ones_sb[0:1, :],
                                      b2_sb[0:1, nh * 512:(nh + 1) * 512],
                                      start=False, stop=True)
                        ev.inc(jx, "pe", 1)
                        ev.mark(f"wo2_{i2}_{grp}", "pe")

        @blk.scalar
        def _(s):
            for i2 in range(nB):
                b, kc = itB(i2)
                par = i2 % 2
                if kc == 0:
                    for dd in range(NKD):
                        ev.wait(s, f"ltT_tr_{b}_{dd}")
                        jx = s.copy(ltT[:, dd * NST:(dd + 1) * NST], ps_trB[0:P, 0:NST])
                        jx = ev.inc(jx, "ac", 1)
                        ev.mark(f"ltT_ev_{b}_{dd}", "ac")
                    ev.mark(f"ltTev_{b}", "ac")
                for j2 in range(32):
                    tt, dd = j2 // NKD, j2 % NKD
                    ev.wait(s, f"y1T_tr_{i2}_{j2}")
                    jx = s.copy(y1T[par][:, dd * TB + tt * P:dd * TB + (tt + 1) * P],
                                ps_trB[:])
                    jx = ev.inc(jx, "ac", 1)
                    ev.mark(f"y1T_ev_{i2}_{j2}", "ac")
                for m in range(NMH):
                    pst = ps_h[m % 2]
                    ev.wait(s, f"hpe_{i2}_{m}")
                    if i2 >= 1 and m == 0:
                        ev.wait(s, f"wo2_{i2 - 1}_7")
                    jx = s.activation(h_sb[:, m * TB:(m + 1) * TB], pst[:], AF.Gelu,
                                      bias=b1_sb[:, m:m + 1])
                    jx = ev.inc(jx, "ac", 1)
                    ev.mark(f"hev_{i2}_{m}", "ac")
                c = ln2col
                for tM in range(4):
                    sl = 8 * (tM % 2)
                    ev.wait(s, f"y2add_{i2}_{tM}")
                    if tM >= 2:
                        ev.wait(s, f"outn_{i2}_{tM - 2}")
                    dmp = lnjB[:]
                    jx = s.activation(dmp, y2pre[tM % 2][:], AF.Square,
                                      accum_out=c[:, sl + 0:sl + 1])
                    s.drain()
                    jx = s.activation(dmp, y2pre[tM % 2][:], AF.Copy,
                                      accum_out=c[:, sl + 1:sl + 2])
                    s.drain()
                    s.activation(c[:, sl + 2:sl + 3], c[:, sl + 1:sl + 2],
                                 AF.Copy, scale=1.0 / DM)
                    s.drain()
                    s.activation(c[:, sl + 3:sl + 4], c[:, sl + 2:sl + 3], AF.Square)
                    s.drain()
                    s.activation(c[:, sl + 4:sl + 5], c[:, sl + 3:sl + 4],
                                 AF.Copy, scale=-1.0, bias=EPS)
                    s.drain()
                    s.activation(c[:, sl + 6:sl + 7], c[:, sl + 0:sl + 1],
                                 AF.Ln, scale=1.0 / DM, bias=c[:, sl + 4:sl + 5])
                    s.drain()
                    jx = s.activation(c[:, sl + 7:sl + 8], c[:, sl + 6:sl + 7],
                                      AF.Exp, scale=-0.5)
                    jx = ev.inc(jx, "ac", 1)
                    ev.mark(f"rstd2_{i2}_{tM}", "ac")

        @blk.vector
        def _(v):
            for i2 in range(nB):
                b, kc = itB(i2)
                par = i2 % 2
                ev.wait(v, f"upe_{i2}")
                v.drain()
                jx = v.tensor_scalar(su_sb[:], ps_ut[0:4, :], scl_sb[0:4, :], None, OP.mult)
                jx = ev.inc(jx, "vc", 1)
                ev.mark(f"su_{i2}", "vc")
                for plo in range(4):
                    ev.wait(v, f"t2pe_{i2}_{plo}")
                    jx = v.tensor_scalar(su2_sb[:, plo * TB:(plo + 1) * TB], ps_ut[0:4, :],
                                         scl_sb[0:4, :], None, OP.mult)
                    jx = ev.inc(jx, "vc", 1)
                    ev.mark(f"su2_{i2}_{plo}", "vc")
                def emit_outn(tM):
                    ev.wait(v, f"rstd2_{i2}_{tM}")
                    if i2 * 4 + tM >= 2:
                        pj = i2 * 4 + tM - 2
                        ev.wait(v, f"outst_{pj // 4}_{pj % 4}")
                    v.drain()
                    jx = v.tensor_scalar(outsb[tM % 2][:], y2pre[tM % 2][:],
                                         ln2col[:, 8 * (tM % 2) + 2:8 * (tM % 2) + 3],
                                         ln2col[:, 8 * (tM % 2) + 7:8 * (tM % 2) + 8],
                                         OP.subtract, OP.mult)
                    jx = ev.inc(jx, "vc", 1)
                    ev.mark(f"outn_{i2}_{tM}", "vc")
                    ev.mark(f"ln2n_{i2}_{tM}", "vc")
                    v.drain()

                for grp in range(8):
                    ps, gq = grp // 4, grp % 4
                    tM = ps * 2 + gq // 2
                    nh = gq % 2
                    ev.wait(v, f"wo2_{i2}_{grp}")
                    if grp >= 1:
                        v.drain()
                    if nh == 0 and (i2 > 0 or tM >= 2):
                        pj = i2 * 4 + tM - 2
                        ev.wait(v, f"ln2n_{pj // 4}_{pj % 4}")
                    jx = v.tensor_tensor(y2pre[tM % 2][:, nh * 512:(nh + 1) * 512],
                                         ps_o[gq][:],
                                         y1cb[par][:, tM * DM + nh * 512:tM * DM + (nh + 1) * 512],
                                         OP.add)
                    jx = ev.inc(jx, "vc", 1)
                    ev.mark(f"o2ev_{i2}_{grp}", "vc")
                    if nh == 1:
                        ev.mark(f"y2add_{i2}_{tM}", "vc")
                        if tM >= 1:
                            emit_outn(tM - 1)
                emit_outn(3)

    stB.close()
    return nc, ev.events


def kernel(L0_attn_x, L0_attn_W_in, L0_attn_conv_w, L0_attn_conv_b, L0_attn_W_x,
           L0_attn_w_dt, L0_attn_b_dt, L0_attn_A_log, L0_attn_D_param, L0_attn_W_out,
           L0_ffn_W1, L0_ffn_b1, L0_ffn_W2, L0_ffn_b2, L0_ffn_scale):
    n_cores = 8
    x = np.ascontiguousarray(np.asarray(L0_attn_x, dtype=np.float32))
    shards = [x[2 * i:2 * i + 2] for i in range(n_cores)]
    common = {
        "W_in": np.ascontiguousarray(L0_attn_W_in, dtype=np.float32),
        "conv_w": np.ascontiguousarray(
            np.asarray(L0_attn_conv_w, dtype=np.float32).reshape(DIN, 4)),
        "conv_b": np.ascontiguousarray(L0_attn_conv_b, dtype=np.float32),
        "W_x": np.ascontiguousarray(L0_attn_W_x, dtype=np.float32),
        "w_dt": np.ascontiguousarray(L0_attn_w_dt, dtype=np.float32),
        "b_dt": np.ascontiguousarray(L0_attn_b_dt, dtype=np.float32),
        "A_log": np.ascontiguousarray(L0_attn_A_log, dtype=np.float32),
        "D_param": np.ascontiguousarray(L0_attn_D_param, dtype=np.float32),
        "W_out": np.ascontiguousarray(L0_attn_W_out, dtype=np.float32),
        "W1": np.ascontiguousarray(L0_ffn_W1, dtype=np.float32),
        "b1": np.ascontiguousarray(L0_ffn_b1, dtype=np.float32),
        "W2": np.ascontiguousarray(L0_ffn_W2, dtype=np.float32),
        "b2": np.ascontiguousarray(L0_ffn_b2, dtype=np.float32),
        "scale": np.ascontiguousarray(
            np.asarray(L0_ffn_scale, dtype=np.float32).reshape(1, 1)),
    }
    _, events = build_nc()
    nc, _ = build_nc(known_events=events)
    in_maps = [dict(common, x=s) for s in shards]
    res = run_bass_kernel_spmd(nc, in_maps, core_ids=list(range(n_cores)))
    return np.concatenate([r["out"] for r in res.results], axis=0)
